# version 03
# baseline (speedup 1.0000x reference)
import numpy as np
import ml_dtypes
import concourse.bass as bass
import concourse.mybir as mybir
from concourse.bass_utils import run_bass_kernel_spmd

F32 = mybir.dt.float32
BF16 = mybir.dt.bfloat16
AF = mybir.ActivationFunctionType
ALU = mybir.AluOpType
AX = mybir.AxisListType

NCORES = 8
TOK = 4096
DM = 2048
DI = 4096
NH = 64
CH = 128
OUT_T = 512
N_OUT = TOK // OUT_T

STAGE = "full"
DBG = []
DBG2 = []
LAST_RES = None

XNH = 0
GU = 32768
ACC = 49152
WS0 = 65536
WS1 = 67584
WU0 = 69632
TMPG = 71680
TMPGC = 72192
TMPU = 72704
WFM = 73216
BFMO = 74496
CFMO = 75008
M0O = 75520
BTO = 75648
AB_SZ = 75776

XTM = 0
XNT = 16384
XBR = 24576
SBF = 41088
DBC = 45184
EALL = 49280
XS = 57472
WIN0 = 61568
WIN1 = 63616
WXP0 = 65664
WXP1 = 69760
WTL0 = 73856
WTL1 = 74176

S0 = 0
XLOAD = 4096
XNTM = 6144
CT0 = 6144
CT1 = 6656
CUMA = 8192
FLT0 = 8704
TMPT = 9728
YDS = 10240
DTRAW = 10752
DT2 = 11264
CUMAT = 11776
SDOT = 11840
DTT = 11904
TMPDS = 11968
DS2 = 12096
DSTT = 12224
CDE2 = 12288
CDED = 12296
CDEB = 12360
WBC = 12424
RL = 12936
M1C = 13064
M2C = 13080
DLT = 13096
EXC = 13112
P1C = 13128
P2C = 13144
EQ1 = 13160
EQ2 = 13168
V1C = 13176
W8A = 13184
MX8 = 13312
AF_SZ = 13320


class Op:
    __slots__ = ("engine", "emit", "deps", "is_dma", "idx", "needed", "rank")

    def __init__(self, engine, emit, deps, is_dma, idx):
        self.engine, self.emit, self.deps, self.is_dma, self.idx = \
            engine, emit, deps, is_dma, idx
        self.needed = False
        self.rank = None


class Buf:
    __slots__ = ("name", "last_w", "readers")

    def __init__(self, name):
        self.name = name
        self.last_w = None
        self.readers = []


class Rec:
    def __init__(self):
        self.ops = []
        self.bufs = {}

    def buf(self, name):
        if name not in self.bufs:
            self.bufs[name] = Buf(name)
        return self.bufs[name]

    def op(self, engine, emit, reads=(), writes=(), dma=False):
        rb = [self.buf(x) for x in reads]
        wb = [self.buf(x) for x in writes]
        deps = set()
        for b in rb:
            if b.last_w is not None:
                deps.add(b.last_w)
        for b in wb:
            if b.last_w is not None:
                deps.add(b.last_w)
            deps.update(b.readers)
        o = Op(engine, emit, deps, dma, len(self.ops))
        self.ops.append(o)
        for b in rb:
            b.readers.append(o)
        for b in wb:
            b.last_w = o
            b.readers = []
        return o

    def fence(self):
        last = {}
        for o in self.ops:
            last[(o.engine, o.is_dma)] = o
        f = Op("dve", lambda e: e.engine_nop(), set(last.values()), False,
               len(self.ops))
        self.ops.append(f)
        for b in self.bufs.values():
            b.last_w = f
            b.readers = []

    SYNC_SAME = ("dve", "act")

    def finalize(self):
        for o in self.ops:
            if o.is_dma:
                o.needed = True
            for d in o.deps:
                if (d.is_dma or d.engine != o.engine
                        or d.engine in self.SYNC_SAME):
                    d.needed = True
        last = {}
        for o in self.ops:
            last[(o.engine, o.is_dma)] = o
        for o in last.values():
            o.needed = True
        cnt = {}
        for o in self.ops:
            if o.needed:
                k = (o.engine, o.is_dma)
                cnt[k] = cnt.get(k, 0) + 1
                o.rank = cnt[k]
        self.final = cnt

    def waits_for(self, o):
        best = {}
        for d in o.deps:
            if not (d.is_dma or d.engine != o.engine
                    or d.engine in self.SYNC_SAME):
                continue
            k = (d.engine, d.is_dma)
            v = d.rank * (16 if d.is_dma else 1)
            if v > best.get(k, -1):
                best[k] = v
        return best


def build_program():
    nc = bass.Bass("TRN2", target_bir_lowering=False)
    E = {}

    def din(name, shape, dtype=F32):
        E[name] = nc.dram_tensor(name, list(shape), dtype, kind="ExternalInput")

    def dint(name, shape, dtype):
        E[name] = nc.dram_tensor(name, list(shape), dtype, kind="Internal")

    din("x", (TOK, DM))
    din("wr", (128, 128))
    din("w_inTz", (DM, DI))
    din("w_inTxb", (DI, DM))
    din("w_xpT", (DI, 4416))
    din("w_outT", (DI, DM))
    din("w_dtT", (64, 64))
    din("wgb", (16384, 2048))
    din("wub", (16384, 2048))
    din("wdb", (16384, 2048))
    din("convw", (128, 128))
    din("convb", (128, 32))
    din("dtb", (64, 1))
    din("alog", (64, 1))
    din("dvec", (64, 1))
    din("negi64", (64, 64))
    din("tri", (128, 128))
    din("id128f", (128, 128))
    din("id128b", (128, 128), BF16)
    din("id64f", (64, 64))
    din("ones64", (64, 128))
    din("ones64b", (64, 128), BF16)
    din("sel8", (8, 1024), BF16)
    din("epsc", (128, 1))
    out_t = nc.dram_tensor("out", [TOK, DM], F32, kind="ExternalOutput")
    dint("s_wgb", (16384, 2048), BF16)
    dint("s_wub", (16384, 2048), BF16)
    dint("s_wdb", (16384, 2048), BF16)
    dint("s_inTz", (DM, DI), BF16)
    dint("s_inTxb", (DI, DM), BF16)
    dint("s_xpT", (DI, 4416), BF16)
    dint("s_outT", (DI, DM), BF16)
    dint("s_ffn", (DM, TOK), BF16)
    dint("s_flat", (64, 512), F32)

    r = Rec()
    op = r.op
    import contextlib
    with contextlib.ExitStack() as es:
        def sbt(name, shape, dtype):
            return es.enter_context(nc.sbuf_tensor(name, list(shape), dtype))

        AB = sbt("AB", [128, AB_SZ], BF16)
        AFt = sbt("AF", [128, AF_SZ], F32)
        c_wr = sbt("c_wr", [128, 128], F32)
        c_negi = sbt("c_negi", [64, 64], F32)
        c_tri = sbt("c_tri", [128, 128], F32)
        c_idf = sbt("c_idf", [128, 128], F32)
        c_idb = sbt("c_idb", [128, 128], BF16)
        c_id64 = sbt("c_id64", [64, 64], F32)
        c_ones64 = sbt("c_ones64", [64, 128], F32)
        c_ones64b = sbt("c_ones64b", [64, 128], BF16)
        c_sel8 = sbt("c_sel8", [8, 1024], BF16)
        c_convw = sbt("c_convw", [128, 128], F32)
        c_convb = sbt("c_convb", [128, 32], F32)
        c_dtb = sbt("c_dtb", [64, 1], F32)
        c_alog = sbt("c_alog", [64, 1], F32)
        c_dvec = sbt("c_dvec", [64, 1], F32)
        negA = sbt("negA", [64, 1], F32)
        c_wdtT = sbt("c_wdtT", [64, 64], F32)
        c_eps = sbt("c_eps", [128, 1], F32)
        haloT = sbt("haloT", [128, 96], BF16)
        stats = sbt("stats", [128, 24], F32)
        lgf = sbt("lgf", [8, 128], F32)

        pA = es.enter_context(nc.psum_tensor("pA", [128, 512], F32))
        pB = es.enter_context(nc.psum_tensor("pB", [128, 512], F32))
        pC = es.enter_context(nc.psum_tensor("pC", [128, 512], F32))
        pD = es.enter_context(nc.psum_tensor("pD", [128, 512], F32))
        pE = es.enter_context(nc.psum_tensor("pE", [128, 512], F32))
        pF = es.enter_context(nc.psum_tensor("pF", [128, 512], F32))
        pG = es.enter_context(nc.psum_tensor("pG", [128, 512], F32))
        pT = es.enter_context(nc.psum_tensor("pT", [128, 1024], BF16))
        PS = {"pA": pA, "pB": pB, "pC": pC, "pD": pD, "pE": pE, "pF": pF,
              "pG": pG, "pT": pT}

        sems = {}
        for n in ("cPE", "cACT", "cDVE", "dSP", "dGP"):
            sems[n] = es.enter_context(nc.semaphore(n))

        def ab(off, cols, p0=0, pn=128):
            return AB[p0:p0 + pn, off:off + cols]

        def af(off, cols, p0=0, pn=128):
            return AFt[p0:p0 + pn, off:off + cols]

        def cast_piece(dstname, srcname, a, b, bufname):
            op("gp", lambda e, a=a, b=b, d=E[dstname], s=E[srcname]:
               e.dma_start(d[a:b, :], s[a:b, :]),
               reads=(), writes=(bufname,), dma=True)

        for e_ in range(8):
            for t_, src, dst in (("g", "wgb", "s_wgb"), ("u", "wub", "s_wub"),
                                 ("d", "wdb", "s_wdb")):
                cast_piece(dst, src, e_ * 2048, (e_ + 1) * 2048, f"sw{t_}{e_}")
        if STAGE == "full":
            for nm, src, rows in (("s_inTxb", "w_inTxb", DI),
                                  ("s_inTz", "w_inTz", DM),
                                  ("s_xpT", "w_xpT", DI),
                                  ("s_outT", "w_outT", DI)):
                for i in range(0, rows, 2048):
                    cast_piece(nm, src, i, min(i + 2048, rows), nm)

        def load(dst_ap, srcname, bufname):
            op("sp", lambda e, d=dst_ap, s=E[srcname]: e.dma_start(d, s[:, :]),
               reads=(), writes=(bufname,), dma=True)

        for t, nm in ((c_wr, "wr"), (c_negi, "negi64"), (c_tri, "tri"),
                      (c_idf, "id128f"), (c_idb, "id128b"), (c_id64, "id64f"),
                      (c_ones64, "ones64"), (c_ones64b, "ones64b"),
                      (c_sel8, "sel8"), (c_convw, "convw"), (c_convb, "convb"),
                      (c_dtb, "dtb"), (c_alog, "alog"), (c_dvec, "dvec"),
                      (c_wdtT, "w_dtT"), (c_eps, "epsc")):
            load(t[:, :], nm, "c_" + nm)

        op("act", lambda e: e.activation(negA[:, :], c_alog[:, :], AF.Exp),
           reads=("c_alog",), writes=("negA",))
        op("dve", lambda e: e.tensor_scalar(negA[:, :], negA[:, :], -1.0, None,
                                            ALU.mult),
           reads=("negA",), writes=("negA",))

        def ln_tile(tok0, router_k, evac, sinkname="gu", par=0):
            sc = par * 12
            stn = f"stats{par}"
            xl = af(XLOAD, 2048)
            op("sp", lambda e: e.dma_start(xl, E["x"][tok0:tok0 + 128, :]),
               reads=(), writes=("xl",), dma=True)
            op("act", lambda e: e.activation(ab(GU, 2048), xl, AF.Square,
                                             accum_out=stats[:, sc + 1:sc + 2]),
               reads=("xl",), writes=(sinkname, stn))
            op("dve", lambda e: e.reduce_sum(stats[:, sc:sc + 1], xl,
                                             axis=AX.X),
               reads=("xl",), writes=(stn,))
            op("dve", lambda e: e.tensor_scalar(
                stats[:, sc + 2:sc + 3], stats[:, sc:sc + 1],
                1.0 / DM, None, ALU.mult),
               reads=(stn,), writes=(stn,))
            op("dve", lambda e: e.tensor_scalar(
                stats[:, sc + 3:sc + 4], stats[:, sc + 1:sc + 2],
                1.0 / DM, None, ALU.mult),
               reads=(stn,), writes=(stn,))
            op("dve", lambda e: e.tensor_tensor(
                stats[:, sc + 4:sc + 5], stats[:, sc + 2:sc + 3],
                stats[:, sc + 2:sc + 3], ALU.mult),
               reads=(stn,), writes=(stn,))
            op("dve", lambda e: e.tensor_tensor(
                stats[:, sc + 5:sc + 6], stats[:, sc + 3:sc + 4],
                stats[:, sc + 4:sc + 5], ALU.subtract),
               reads=(stn,), writes=(stn,))
            op("act", lambda e: e.activation(
                stats[:, sc + 6:sc + 7], stats[:, sc + 5:sc + 6],
                AF.Sqrt, bias=c_eps[:, 0:1]),
               reads=(stn, "c_epsc"), writes=(stn,))
            op("dve", lambda e: e.reciprocal(stats[:, sc + 7:sc + 8],
                                             stats[:, sc + 6:sc + 7]),
               reads=(stn,), writes=(stn,))
            op("dve", lambda e: e.tensor_scalar(
                af(XNTM, 2048), xl, stats[:, sc + 2:sc + 3],
                stats[:, sc + 7:sc + 8], ALU.subtract, ALU.mult),
               reads=("xl", stn), writes=("xntm",))
            for k in range(16):
                p, pn = (pA, "pA") if k % 2 == 0 else (pB, "pB")
                op("pe", lambda e, k=k, p=p: e.transpose(
                    p[:, 0:128], af(XNTM + 128 * k, 128), c_idf[:, :]),
                   reads=("xntm", "c_id128f"), writes=(pn,))
                if router_k:
                    op("dve", lambda e, p=p: e.tensor_copy(af(WBC, 128),
                                                           p[:, 0:128]),
                       reads=(pn,), writes=("wbc",))
                    op("pe", lambda e, k=k: e.matmul(
                        pG[0:8, 0:128], c_wr[:, 8 * k:8 * k + 8], af(WBC, 128),
                        start=(k == 0), stop=(k == 15)),
                       reads=("wbc", "c_wr"), writes=("pG",))
                evac(k, p, pn)

        for half in range(2):
            ht0 = half * 2048
            for j in range(16):
                def mo_evac(k, p, pn, j=j):
                    op("dve", lambda e, k=k, j=j, p=p: e.tensor_copy(
                        ab(XNH + k * 2048 + j * 128, 128), p[:, 0:128]),
                       reads=(pn,), writes=("xnh",))
                ln_tile(ht0 + j * 128, True, mo_evac, par=j % 2)
                op("dve", lambda e: e.tensor_copy(lgf[:, :], pG[0:8, 0:128]),
                   reads=("pG",), writes=("lgf",))
                op("pe", lambda e: e.transpose(pG[0:128, 0:8], lgf[:, :],
                                               c_idf[0:8, 0:8]),
                   reads=("lgf", "c_id128f"), writes=("pG",))
                op("dve", lambda e, j=j: e.tensor_copy(af(RL + 8 * j, 8),
                                                       pG[0:128, 0:8]),
                   reads=("pG",), writes=("rtr",))
                op("dve", lambda e, j=j: e.max(out=af(MX8, 8),
                                               in_=af(RL + 8 * j, 8)),
                   reads=("rtr",), writes=("rtr",))
                op("dve", lambda e, j=j: e.tensor_copy(af(M1C + j, 1),
                                                       af(MX8, 1)),
                   reads=("rtr",), writes=("rtr",))
                op("dve", lambda e, j=j: e.tensor_copy(af(M2C + j, 1),
                                                       af(MX8 + 1, 1)),
                   reads=("rtr",), writes=("rtr",))
                op("dve", lambda e, j=j: e.tensor_tensor(
                    af(DLT + j, 1), af(M2C + j, 1), af(M1C + j, 1),
                    ALU.subtract),
                   reads=("rtr",), writes=("rtr",))
            op("act", lambda e: e.activation(af(EXC, 16), af(DLT, 16), AF.Exp),
               reads=("rtr",), writes=("rtr",))
            op("dve", lambda e: e.tensor_scalar(af(EXC, 16), af(EXC, 16), 1.0,
                                                None, ALU.add),
               reads=("rtr",), writes=("rtr",))
            op("dve", lambda e: e.reciprocal(af(P1C, 16), af(EXC, 16)),
               reads=("rtr",), writes=("rtr",))
            op("dve", lambda e: e.tensor_scalar(af(P2C, 16), af(P1C, 16), -1.0,
                                                1.0, ALU.mult, ALU.add),
               reads=("rtr",), writes=("rtr",))
            for j in range(16):
                op("dve", lambda e, j=j: e.tensor_single_scalar(
                    af(EQ1, 8), af(RL + 8 * j, 8), af(M1C + j, 1),
                    ALU.is_equal),
                   reads=("rtr",), writes=("rtr8",))
                op("dve", lambda e, j=j: e.tensor_single_scalar(
                    af(EQ2, 8), af(RL + 8 * j, 8), af(M2C + j, 1),
                    ALU.is_equal),
                   reads=("rtr", "rtr8"), writes=("rtr8",))
                op("dve", lambda e, j=j: e.tensor_scalar(
                    af(V1C, 8), af(EQ1, 8), af(P1C + j, 1), None, ALU.mult),
                   reads=("rtr8", "rtr"), writes=("rtr8",))
                op("dve", lambda e, j=j: e.scalar_tensor_tensor(
                    af(W8A + 8 * j, 8), af(EQ2, 8), af(P2C + j, 1),
                    af(V1C, 8), ALU.mult, ALU.add),
                   reads=("rtr8", "rtr"), writes=("w8",))
                op("pe", lambda e, j=j: e.transpose(
                    pG[0:8, 0:128], af(W8A + 8 * j, 8), c_idf[:, :]),
                   reads=("w8", "c_id128f"), writes=("pG",))
                op("dve", lambda e, j=j: e.tensor_copy(
                    ab(WFM + j * 128, 128, 0, 8), pG[0:8, 0:128]),
                   reads=("pG",), writes=("wfm",))

            for tbD in range(2):
                t0 = tbD * 1024
                for e_ in range(8):
                    for t5 in range(2):
                        tt0 = t0 + t5 * 512
                        wb_off = WBC if t5 == 0 else CUMA
                        wbn = f"wbc{t5}"
                        op("pe", lambda e, e_=e_, tt0=tt0: e.matmul(
                            pG[:, 0:512],
                            c_sel8[:, e_ * 128:(e_ + 1) * 128],
                            ab(WFM + tt0, 512, 0, 8),
                            start=True, stop=True),
                           reads=("wfm", "c_sel8"), writes=("pG",))
                        op("dve", lambda e, wb_off=wb_off: e.tensor_copy(
                            af(wb_off, 512), pG[:, 0:512]),
                           reads=("pG",), writes=(wbn,))
                    for ic in range(16):
                        ws = WS0 if ic % 2 == 0 else WS1
                        wsn = f"wg{ic % 2}"
                        op("sp", lambda e, e_=e_, ic=ic, ws=ws: e.dma_start(
                            ab(ws, 2048),
                            E["s_wgb"][e_ * 2048 + ic * 128:
                                       e_ * 2048 + ic * 128 + 128, :]),
                           reads=(f"swg{e_}",), writes=(wsn,), dma=True)
                        op("sp", lambda e, e_=e_, ic=ic: e.dma_start(
                            ab(WU0, 2048),
                            E["s_wub"][e_ * 2048 + ic * 128:
                                       e_ * 2048 + ic * 128 + 128, :]),
                           reads=(f"swu{e_}",), writes=("wu0",), dma=True)
                        for k in range(16):
                            for t5 in range(2):
                                pg, pgn = (pA, "pA") if t5 == 0 else (pB, "pB")
                                op("pe", lambda e, k=k, ws=ws, t5=t5, pg=pg,
                                   t0=t0: e.matmul(
                                       pg[:, 0:512],
                                       ab(ws + k * 128, 128),
                                       ab(XNH + k * 2048 + t0 + t5 * 512, 512),
                                       start=(k == 0), stop=(k == 15)),
                                   reads=(wsn, "xnh"), writes=(pgn,))
                        for t5 in range(2):
                            pg, pgn = (pA, "pA") if t5 == 0 else (pB, "pB")
                            tg = TMPG if t5 == 0 else TMPU
                            tgn = f"tmpg{t5}"
                            op("act", lambda e, pg=pg, tg=tg: e.activation(
                                ab(tg, 512), pg[:, 0:512], AF.Silu),
                               reads=(pgn,), writes=(tgn,))
                            op("dve", lambda e, tg=tg: e.tensor_scalar(
                                ab(tg, 512), ab(tg, 512), 7.0, -7.0,
                                ALU.min, ALU.max),
                               reads=(tgn,), writes=(tgn,))
                        for k in range(16):
                            for t5 in range(2):
                                pu = pC if t5 == 0 else pD
                                pun = "pC" if t5 == 0 else "pD"
                                op("pe", lambda e, k=k, t5=t5, pu=pu, t0=t0:
                                   e.matmul(
                                       pu[:, 0:512],
                                       ab(WU0 + k * 128, 128),
                                       ab(XNH + k * 2048 + t0 + t5 * 512, 512),
                                       start=(k == 0), stop=(k == 15)),
                                   reads=("wu0", "xnh"), writes=(pun,))
                        for t5 in range(2):
                            pu = pC if t5 == 0 else pD
                            pun = "pC" if t5 == 0 else "pD"
                            tg = TMPG if t5 == 0 else TMPU
                            tgn = f"tmpg{t5}"
                            wb_off = WBC if t5 == 0 else CUMA
                            wbn = f"wbc{t5}"
                            op("dve", lambda e, pu=pu, wb_off=wb_off:
                               e.tensor_tensor(
                                   ab(TMPGC, 512), pu[:, 0:512],
                                   af(wb_off, 512), ALU.mult),
                               reads=(pun, wbn), writes=("tmpgc",))
                            op("dve", lambda e, ic=ic, t5=t5, tg=tg:
                               e.tensor_tensor(
                                   ab(GU + ic * 1024 + t5 * 512, 512),
                                   ab(tg, 512), ab(TMPGC, 512), ALU.mult),
                               reads=(tgn, "tmpgc"), writes=("gu",))
                    for oc in range(16):
                        ws = WS0 if oc % 2 == 0 else WS1
                        wsn = f"wg{oc % 2}"
                        op("sp", lambda e, e_=e_, oc=oc, ws=ws: e.dma_start(
                            ab(ws, 2048),
                            E["s_wdb"][e_ * 2048 + oc * 128:
                                       e_ * 2048 + oc * 128 + 128, :]),
                           reads=(f"swd{e_}",), writes=(wsn,), dma=True)
                        for ic in range(16):
                            for t5 in range(2):
                                pd_ = pE if t5 == 0 else pF
                                pdn = "pE" if t5 == 0 else "pF"
                                op("pe", lambda e, ic=ic, t5=t5, ws=ws,
                                   pd_=pd_: e.matmul(
                                       pd_[:, 0:512], ab(ws + ic * 128, 128),
                                       ab(GU + ic * 1024 + t5 * 512, 512),
                                       start=(ic == 0), stop=(ic == 15)),
                                   reads=(wsn, "gu"), writes=(pdn,))
                        for t5 in range(2):
                            pd_ = pE if t5 == 0 else pF
                            pdn = "pE" if t5 == 0 else "pF"
                            if e_ == 0:
                                op("dve", lambda e, oc=oc, t5=t5, pd_=pd_:
                                   e.tensor_copy(
                                       ab(ACC + oc * 1024 + t5 * 512, 512),
                                       pd_[:, 0:512]),
                                   reads=(pdn,), writes=("acc",))
                            else:
                                op("dve", lambda e, oc=oc, t5=t5, pd_=pd_:
                                   e.tensor_tensor(
                                       ab(ACC + oc * 1024 + t5 * 512, 512),
                                       ab(ACC + oc * 1024 + t5 * 512, 512),
                                       pd_[:, 0:512], ALU.add),
                                   reads=("acc", pdn), writes=("acc",))
                c0 = ht0 + t0
                dst = bass.AP(E["s_ffn"], c0,
                              [[4096, 128], [128 * 4096, 16], [1, 1024]])
                op("gp", lambda e, dst=dst: e.dma_start(dst, ab(ACC, 16384)),
                   reads=("acc",), writes=("sffn",), dma=True)

        if DBG:
            allb0 = tuple(r.bufs.keys())
            for i, (ar, off, cols, p0, pn) in enumerate(DBG):
                dbg_t = nc.dram_tensor(f"dbg{i}", [pn, cols], F32,
                                       kind="ExternalOutput")
                src = (ab(off, cols, p0, pn) if ar == "ab"
                       else af(off, cols, p0, pn))
                op("gp", lambda e, d=dbg_t, s_=src: e.dma_start(d[:, :], s_),
                   reads=allb0, writes=(f"dbg{i}",), dma=True)

        if STAGE == "full":
            r.fence()

            op("dve", lambda e: e.tensor_scalar(
                ab(EALL, 4096, 0, 64),
                bass.AP(c_id64, 0, [[64, 64], [1, 64], [0, 64]]),
                c_dvec[:, 0:1], None, ALU.mult),
               reads=("c_id64f", "c_dvec"), writes=("eall",))
            for q in range(8):
                op("pe", lambda e, q=q: e.matmul(
                    pA[:, 0:512], c_ones64b[:, :], ab(EALL + 512 * q, 512,
                                                      0, 64),
                    start=True, stop=True),
                   reads=("eall", "c_ones64b"), writes=("pA",))
                op("dve", lambda e, q=q: e.tensor_copy(ab(DBC + 512 * q, 512),
                                                       pA[:, 0:512]),
                   reads=("pA",), writes=("dbc",))

            for o_ in range(N_OUT):
                ot0 = o_ * OUT_T
                seq_start = (o_ % 4 == 0)
                for j in range(4):
                    def ma_evac(k, p, pn, j=j):
                        op("dve", lambda e, k=k, j=j, p=p: e.tensor_copy(
                            ab(XNT + k * 512 + j * 128, 128), p[:, 0:128]),
                           reads=(pn,), writes=("xnt",))
                    ln_tile(ot0 + j * 128, False, ma_evac,
                            sinkname="xbr", par=j % 2)

                if seq_start:
                    op("dve", lambda e: e.memset(haloT[:, :], 0.0),
                       reads=(), writes=("halo",))
                for ch in range(32):
                    op("dve", lambda e, ch=ch: e.tensor_copy(
                        ab(XBR + 516 * ch + 1, 3), haloT[:, 3 * ch:3 * ch + 3]),
                       reads=("halo",), writes=("xbr",))

                for oc in range(32):
                    ws = WIN0 if oc % 2 == 0 else WIN1
                    wsn = f"win{oc % 2}"
                    op("sp", lambda e, oc=oc, ws=ws: e.dma_start(
                        ab(ws, 2048),
                        E["s_inTxb"][oc * 128:(oc + 1) * 128, :]),
                       reads=("s_inTxb",), writes=(wsn,), dma=True)
                    p, pn = (pA, "pA") if oc % 2 == 0 else (pB, "pB")
                    for k in range(16):
                        op("pe", lambda e, k=k, ws=ws, p=p: e.matmul(
                            p[:, 0:512], ab(ws + k * 128, 128),
                            ab(XNT + k * 512, 512),
                            start=(k == 0), stop=(k == 15)),
                           reads=(wsn, "xnt"), writes=(pn,))
                    op("dve", lambda e, ch=oc, p=p: e.tensor_copy(
                        ab(XBR + 516 * ch + 4, 512), p[:, 0:512]),
                       reads=(pn,), writes=("xbr",))

                for ch in range(32):
                    op("dve", lambda e, ch=ch: e.tensor_copy(
                        haloT[:, 3 * ch:3 * ch + 3],
                        ab(XBR + 516 * ch + 513, 3)),
                       reads=("xbr",), writes=("halo",))
                    op("dve", lambda e, ch=ch: e.tensor_scalar(
                        af(CT0, 512), ab(XBR + 516 * ch + 1, 512),
                        c_convw[:, 4 * ch:4 * ch + 1], None, ALU.mult),
                       reads=("xbr", "c_convw"), writes=("xntm",))
                    for k in range(1, 4):
                        src, dst = (CT0, CT1) if k % 2 == 1 else (CT1, CT0)
                        op("dve", lambda e, ch=ch, k=k, src=src, dst=dst:
                           e.scalar_tensor_tensor(
                               af(dst, 512), ab(XBR + 516 * ch + 1 + k, 512),
                               c_convw[:, 4 * ch + k:4 * ch + k + 1],
                               af(src, 512), ALU.mult, ALU.add),
                           reads=("xbr", "c_convw", "xntm"), writes=("xntm",))
                    op("act", lambda e, ch=ch: e.activation(
                        ab(XBR + 516 * ch + 4, 512), af(CT1, 512), AF.Silu,
                        bias=c_convb[:, ch:ch + 1]),
                       reads=("xntm", "c_convb"), writes=("xbr",))

                for osl in range(8):
                    for kg in range(4):
                        ws = WXP0 if kg % 2 == 0 else WXP1
                        wsn = f"wxp{kg % 2}"
                        src = bass.AP(
                            E["s_xpT"], kg * 1024 * 4416 + osl * 512,
                            [[4416, 128], [128 * 4416, 8], [1, 512]])
                        op("sp", lambda e, ws=ws, src=src: e.dma_start(
                            ab(ws, 4096), src),
                           reads=("s_xpT",), writes=(wsn,), dma=True)
                        for k8 in range(8):
                            k = kg * 8 + k8
                            for tl in range(4):
                                pp = (pC, pD, pE, pF)[tl]
                                ppn = ("pC", "pD", "pE", "pF")[tl]
                                op("pe", lambda e, k=k, k8=k8, tl=tl, ws=ws,
                                   pp=pp: e.matmul(
                                       pp[:, 0:512],
                                       ab(XBR + 516 * k + 4 + tl * 128, 128),
                                       ab(ws + k8 * 512, 512),
                                       start=(k == 0), stop=(k == 31)),
                                   reads=(wsn, "xbr"), writes=(ppn,))
                    for tl in range(4):
                        pp = (pC, pD, pE, pF)[tl]
                        ppn = ("pC", "pD", "pE", "pF")[tl]
                        op("dve", lambda e, tl=tl, osl=osl, pp=pp:
                           e.tensor_copy(
                               ab(XTM + tl * 4096 + osl * 512, 512),
                               pp[:, 0:512]),
                           reads=(ppn,), writes=("xtm",))

                kgroups = [list(range(g, min(g + 3, 32)))
                           for g in range(0, 32, 3)]
                for gi, grp in enumerate(kgroups):
                    ws = WTL0 if gi % 2 == 0 else WTL1
                    wsn = f"wtl{gi % 2}"
                    k0, nk = grp[0], len(grp)
                    src = bass.AP(E["s_xpT"], k0 * 128 * 4416 + 4096,
                                  [[4416, 128], [128 * 4416, nk], [1, 320]])
                    op("sp", lambda e, ws=ws, src=src, nk=nk: e.dma_start(
                        ab(ws, nk * 320), src),
                       reads=("s_xpT",), writes=(wsn,), dma=True)
                    for ji, k in enumerate(grp):
                        wo = ws + ji * 320
                        op("pe", lambda e, k=k, wo=wo: e.matmul(
                            pA[0:64, 0:512], ab(wo, 64),
                            ab(XBR + 516 * k + 4, 512),
                            start=(k == 0), stop=(k == 31)),
                           reads=(wsn, "xbr"), writes=("pA",))
                        op("pe", lambda e, k=k, wo=wo: e.matmul(
                            pB[:, 0:512], ab(wo + 64, 128),
                            ab(XBR + 516 * k + 4, 512),
                            start=(k == 0), stop=(k == 31)),
                           reads=(wsn, "xbr"), writes=("pB",))
                        op("pe", lambda e, k=k, wo=wo: e.matmul(
                            pG[:, 0:512], ab(wo + 192, 128),
                            ab(XBR + 516 * k + 4, 512),
                            start=(k == 0), stop=(k == 31)),
                           reads=(wsn, "xbr"), writes=("pG",))
                op("dve", lambda e: e.tensor_copy(af(DTRAW, 512, 0, 64),
                                                  pA[0:64, 0:512]),
                   reads=("pA",), writes=("dtraw",))
                op("dve", lambda e: e.tensor_copy(ab(BFMO, 512), pB[:, 0:512]),
                   reads=("pB",), writes=("bfm",))
                op("dve", lambda e: e.tensor_copy(ab(CFMO, 512), pG[:, 0:512]),
                   reads=("pG",), writes=("cfm",))

                op("pe", lambda e: e.matmul(pC[0:64, 0:512], c_wdtT[:, :],
                                            af(DTRAW, 512, 0, 64),
                                            start=True, stop=True),
                   reads=("dtraw", "c_w_dtT"), writes=("pC",))
                op("act", lambda e: e.activation(af(TMPT, 512, 0, 64),
                                                 pC[0:64, 0:512], AF.Abs,
                                                 bias=c_dtb[:, 0:1]),
                   reads=("pC", "c_dtb"), writes=("tmpt",))
                op("act", lambda e: e.activation(af(YDS, 512, 0, 64),
                                                 af(TMPT, 512, 0, 64), AF.Exp,
                                                 scale=-1.0),
                   reads=("tmpt",), writes=("yds",))
                op("act", lambda e: e.activation(af(TMPT, 512, 0, 64),
                                                 af(YDS, 512, 0, 64), AF.Ln,
                                                 bias=1.0),
                   reads=("yds",), writes=("tmpt",))
                op("act", lambda e: e.activation(af(YDS, 512, 0, 64),
                                                 pC[0:64, 0:512], AF.Relu,
                                                 bias=c_dtb[:, 0:1]),
                   reads=("pC", "c_dtb"), writes=("yds",))
                op("dve", lambda e: e.tensor_tensor(af(DT2, 512, 0, 64),
                                                    af(TMPT, 512, 0, 64),
                                                    af(YDS, 512, 0, 64),
                                                    ALU.add),
                   reads=("tmpt", "yds"), writes=("dt2",))
                op("dve", lambda e: e.tensor_scalar(af(DTRAW, 512, 0, 64),
                                                    af(DT2, 512, 0, 64),
                                                    negA[:, 0:1], None,
                                                    ALU.mult),
                   reads=("dt2", "negA"), writes=("dtraw",))
                for c in range(4):
                    op("dve", lambda e, c=c: e.tensor_tensor_scan(
                        af(CUMA + 128 * c, 128, 0, 64),
                        af(DTRAW + 128 * c, 128, 0, 64),
                        af(DTRAW + 128 * c, 128, 0, 64), 0.0,
                        ALU.add, ALU.bypass),
                       reads=("dtraw",), writes=("cuma",))
                op("gp", lambda e: e.dma_start(E["s_flat"][:, :],
                                               af(CUMA, 512, 0, 64)),
                   reads=("cuma",), writes=("sflat",), dma=True)

                if seq_start:
                    op("dve", lambda e: e.memset(af(S0, 4096), 0.0),
                       reads=(), writes=("S",))
                    op("dve", lambda e: e.memset(ab(SBF, 4096), 0.0),
                       reads=(), writes=("sbf",))

                for c in range(4):
                    for js in range(16):
                        if js % 2 == 0:
                            q = js // 2
                            src = bass.AP(E["s_flat"],
                                          8 * q * 512 + 128 * c,
                                          [[1, 1], [512, 8], [1, 128]])
                            op("gp", lambda e, src=src: e.dma_start(
                                af(FLT0, 1024, 0, 1), src),
                               reads=("sflat",), writes=("flt",), dma=True)
                        p, pn = (pA, "pA") if js % 2 == 0 else (pB, "pB")
                        rhs_negi = bass.AP(c_negi, 4 * js,
                                           [[64, 64], [1, 4], [0, 128]])
                        op("pe", lambda e, c=c, p=p, rn=rhs_negi: e.matmul(
                            p[:, 0:512], af(CUMA + 128 * c, 128, 0, 64), rn,
                            start=True, stop=False),
                           reads=("cuma", "c_negi64"), writes=(pn,))
                        op("pe", lambda e, js=js, p=p: e.matmul(
                            p[:, 0:512], c_ones64[0:1, :],
                            af(FLT0 + (js % 2) * 512, 512, 0, 1),
                            start=False, stop=True),
                           reads=("flt", "c_ones64"), writes=(pn,))
                        op("dve", lambda e, p=p: e.tensor_tensor(
                            af(TMPT, 512), p[:, 0:512],
                            bass.AP(c_tri, 0, [[128, 128], [0, 4], [1, 128]]),
                            ALU.add),
                           reads=(pn, "c_tri"), writes=("tmpt",))
                        op("act", lambda e, js=js: e.activation(
                            ab(EALL + 512 * js, 512), af(TMPT, 512), AF.Exp),
                           reads=("tmpt",), writes=("eall",))
                    op("pe", lambda e, c=c: e.transpose(
                        pC[:, 0:64], af(CUMA + 128 * c, 128, 0, 64),
                        c_id64[:, :]),
                       reads=("cuma", "c_id64f"), writes=("pC",))
                    op("dve", lambda e: e.tensor_copy(af(CUMAT, 64),
                                                      pC[:, 0:64]),
                       reads=("pC",), writes=("cumat",))
                    op("act", lambda e: e.activation(af(SDOT, 64),
                                                     af(CUMAT, 64), AF.Exp),
                       reads=("cumat",), writes=("sdot",))
                    op("pe", lambda e, c=c: e.transpose(
                        pC[:, 64:128], af(DT2 + 128 * c, 128, 0, 64),
                        c_id64[:, :]),
                       reads=("dt2", "c_id64f"), writes=("pC",))
                    op("dve", lambda e: e.tensor_copy(af(DTT, 64),
                                                      pC[:, 64:128]),
                       reads=("pC",), writes=("dtt",))
                    op("dve", lambda e, c=c: e.tensor_scalar(
                        af(TMPDS, 128, 0, 64), af(CUMA + 128 * c, 128, 0, 64),
                        -1.0, af(CUMA + 128 * c + 127, 1, 0, 64),
                        ALU.mult, ALU.add),
                       reads=("cuma",), writes=("tmpds",))
                    op("act", lambda e: e.activation(af(DS2, 128, 0, 64),
                                                     af(TMPDS, 128, 0, 64),
                                                     AF.Exp),
                       reads=("tmpds",), writes=("ds2",))
                    op("pe", lambda e: e.transpose(pD[:, 0:64],
                                                   af(DS2, 128, 0, 64),
                                                   c_id64[:, :]),
                       reads=("ds2", "c_id64f"), writes=("pD",))
                    op("dve", lambda e: e.tensor_copy(af(DSTT, 64),
                                                      pD[:, 0:64]),
                       reads=("pD",), writes=("dstt",))
                    op("act", lambda e, c=c: e.activation(
                        af(CDE2, 1, 0, 64), af(CUMA + 128 * c + 127, 1, 0, 64),
                        AF.Exp),
                       reads=("cuma",), writes=("cde2",))
                    op("dve", lambda e: e.tensor_scalar(
                        af(CDED, 64, 0, 64), c_id64[:, :], af(CDE2, 1, 0, 64),
                        None, ALU.mult),
                       reads=("cde2", "c_id64f"), writes=("cded",))
                    op("pe", lambda e: e.matmul(pD[:, 64:128], c_ones64[:, :],
                                                af(CDED, 64, 0, 64),
                                                start=True, stop=True),
                       reads=("cded", "c_ones64"), writes=("pD",))
                    op("dve", lambda e: e.tensor_copy(af(CDEB, 64),
                                                      pD[:, 64:128]),
                       reads=("pD",), writes=("cdeb",))
                    op("pe", lambda e, c=c: e.matmul(
                        pD[:, 128:256], ab(BFMO + 128 * c, 128),
                        ab(CFMO + 128 * c, 128), start=True, stop=True),
                       reads=("bfm", "cfm"), writes=("pD",))
                    op("dve", lambda e: e.tensor_copy(ab(M0O, 128), pD[:, 128:256]),
                       reads=("pD",), writes=("m0",))
                    op("pe", lambda e, c=c: e.transpose(
                        pT[:, 0:128], ab(BFMO + 128 * c, 128),
                        c_idb[:, :]),
                       reads=("bfm", "c_id128b"), writes=("pT",))
                    op("dve", lambda e: e.tensor_copy(ab(BTO, 128), pT[:, 0:128]),
                       reads=("pT",), writes=("bt",))

                    for g8 in range(8):
                        pyd, pydn = (pE, "pE") if g8 % 2 == 0 else (pF, "pF")
                        hs = [8 * g8 + i for i in range(8)]
                        for h in hs:
                            op("dve", lambda e, h=h: e.tensor_tensor(
                                ab(EALL + 128 * h, 128), ab(EALL + 128 * h, 128),
                                ab(M0O, 128), ALU.mult),
                               reads=("eall", "m0"), writes=("eall",))
                            op("dve", lambda e, h=h, c=c: e.tensor_scalar(
                                ab(XS + 64 * h, 64),
                                ab(XTM + c * 4096 + 64 * h, 64),
                                af(DTT + h, 1), None, ALU.mult),
                               reads=("xtm", "dtt"), writes=("xs",))
                        for h in hs:
                            op("pe", lambda e, h=h, pyd=pyd: e.matmul(
                                pyd[:, 64 * (h % 8):64 * (h % 8) + 64],
                                ab(EALL + 128 * h, 128), ab(XS + 64 * h, 64),
                                start=True, stop=True),
                               reads=("eall", "xs"), writes=(pydn,))
                        for h in hs:
                            op("act", lambda e, h=h, pyd=pyd: e.activation(
                                af(YDS + 64 * (h % 8), 64),
                                pyd[:, 64 * (h % 8):64 * (h % 8) + 64],
                                AF.Identity),
                               reads=(pydn,), writes=("yds",))
                        for h in hs:
                            op("dve", lambda e, h=h: e.tensor_scalar(
                                ab(XS + 64 * h, 64), ab(XS + 64 * h, 64),
                                af(DSTT + h, 1), None, ALU.mult),
                               reads=("xs", "dstt"), writes=("xs",))
                        for g4 in range(2):
                            hq = hs[4 * g4:4 * g4 + 4]
                            op("pe", lambda e, c=c, g4=g4, hq=hq: e.matmul(
                                pG[:, 256 * g4:256 * g4 + 256],
                                ab(CFMO + 128 * c, 128),
                                ab(SBF + 64 * hq[0], 256),
                                start=True, stop=True),
                               reads=("cfm", "sbf"), writes=("pG",))
                            for h in hq:
                                op("dve", lambda e, h=h, c=c, g4=g4:
                                   e.scalar_tensor_tensor(
                                       ab(XBR + c * 4096 + 64 * h, 64),
                                       pG[:, 256 * g4 + 64 * (h % 4):
                                          256 * g4 + 64 * (h % 4) + 64],
                                       af(SDOT + h, 1),
                                       af(YDS + 64 * (h % 8), 64),
                                       ALU.mult, ALU.add),
                                   reads=("pG", "sdot", "yds"),
                                   writes=("xbr",))
                            pm = pC if g4 == 0 else pD
                            pmn = "pC" if g4 == 0 else "pD"
                            op("pe", lambda e, hq=hq, pm=pm: e.matmul(
                                pm[:, 256:512], ab(BTO, 128),
                                ab(XS + 64 * hq[0], 256),
                                start=True, stop=True),
                               reads=("bt", "xs"), writes=(pmn,))
                            for h in hq:
                                op("dve", lambda e, h=h, pm=pm:
                                   e.scalar_tensor_tensor(
                                       af(S0 + 64 * h, 64), af(S0 + 64 * h, 64),
                                       af(CDEB + h, 1),
                                       pm[:, 256 + 64 * (h % 4):
                                          256 + 64 * (h % 4) + 64],
                                       ALU.mult, ALU.add),
                                   reads=("S", "cdeb", pmn), writes=("S",))
                    op("act", lambda e: e.activation(ab(SBF, 2048),
                                                     af(S0, 2048), AF.Identity),
                       reads=("S",), writes=("sbf",))
                    op("act", lambda e: e.activation(ab(SBF + 2048, 2048),
                                                     af(S0 + 2048, 2048),
                                                     AF.Identity),
                       reads=("S",), writes=("sbf",))

                for j in range(4):
                    tok0 = ot0 + j * 128
                    for osl in range(8):
                        pz, pzn = (pA, "pA") if osl % 2 == 0 else (pB, "pB")
                        for kg in range(2):
                            ws = WXP0 if kg % 2 == 0 else WXP1
                            wsn = f"wxp{kg % 2}"
                            src = bass.AP(
                                E["s_inTz"], kg * 1024 * 4096 + osl * 512,
                                [[4096, 128], [128 * 4096, 8], [1, 512]])
                            op("sp", lambda e, ws=ws, src=src: e.dma_start(
                                ab(ws, 4096), src),
                               reads=("s_inTz",), writes=(wsn,), dma=True)
                            for k8 in range(8):
                                k = kg * 8 + k8
                                op("pe", lambda e, k=k, k8=k8, j=j, ws=ws,
                                   pz=pz: e.matmul(
                                       pz[:, 0:512],
                                       ab(XNT + k * 512 + j * 128, 128),
                                       ab(ws + k8 * 512, 512),
                                       start=(k == 0), stop=(k == 15)),
                                   reads=(wsn, "xnt"), writes=(pzn,))
                        op("act", lambda e, osl=osl, pz=pz: e.activation(
                            ab(XS + osl * 512, 512), pz[:, 0:512], AF.Silu),
                           reads=(pzn,), writes=("xs",))
                    op("dve", lambda e, j=j: e.tensor_tensor(
                        ab(EALL, 4096), ab(XTM + j * 4096, 4096),
                        ab(DBC, 4096), ALU.mult),
                       reads=("xtm", "dbc"), writes=("eall",))
                    op("dve", lambda e, j=j: e.tensor_tensor(
                        ab(EALL, 4096), ab(EALL, 4096),
                        ab(XBR + j * 4096, 4096), ALU.add),
                       reads=("eall", "xbr"), writes=("eall",))
                    op("dve", lambda e: e.tensor_tensor(
                        ab(EALL, 4096), ab(EALL, 4096), ab(XS, 4096),
                        ALU.mult),
                       reads=("eall", "xs"), writes=("eall",))
                    for k in range(32):
                        sl = k % 8
                        op("pe", lambda e, k=k, sl=sl: e.transpose(
                            pT[:, 128 * sl:128 * sl + 128],
                            ab(EALL + 128 * k, 128), c_idb[:, :]),
                           reads=("eall", "c_id128b"), writes=("pT",))
                        op("dve", lambda e, k=k, sl=sl: e.tensor_copy(
                            ab(EALL + 4096 + 128 * k, 128),
                            pT[:, 128 * sl:128 * sl + 128]),
                           reads=("pT",), writes=("eall",))
                    op("sp", lambda e, tok0=tok0: e.dma_start(
                        ab(WIN0, 2048), E["s_ffn"][:, tok0:tok0 + 128],
                        transpose=True),
                       reads=("sffn",), writes=("win0",), dma=True)
                    op("sp", lambda e, tok0=tok0: e.dma_start(
                        af(XLOAD, 2048), E["x"][tok0:tok0 + 128, :]),
                       reads=(), writes=("xl",), dma=True)
                    for osl in range(4):
                        po, pon = (pC, "pC") if osl % 2 == 0 else (pD, "pD")
                        for kg in range(4):
                            ws = WXP0 if kg % 2 == 0 else WXP1
                            wsn = f"wxp{kg % 2}"
                            src = bass.AP(
                                E["s_outT"], kg * 1024 * 2048 + osl * 512,
                                [[2048, 128], [128 * 2048, 8], [1, 512]])
                            op("sp", lambda e, ws=ws, src=src: e.dma_start(
                                ab(ws, 4096), src),
                               reads=("s_outT",), writes=(wsn,), dma=True)
                            for k8 in range(8):
                                k = kg * 8 + k8
                                op("pe", lambda e, k=k, k8=k8, ws=ws, po=po:
                                   e.matmul(
                                       po[:, 0:512],
                                       ab(EALL + 4096 + 128 * k, 128),
                                       ab(ws + k8 * 512, 512),
                                       start=(k == 0), stop=(k == 31)),
                                   reads=(wsn, "eall"), writes=(pon,))
                        op("dve", lambda e, osl=osl, po=po:
                           e.scalar_tensor_tensor(
                               af(XNTM + osl * 512, 512), po[:, 0:512], 1.0,
                               ab(WIN0 + osl * 512, 512),
                               ALU.mult, ALU.add),
                           reads=(pon, "win0"), writes=("xntm",))
                        op("dve", lambda e, osl=osl: e.tensor_tensor(
                            af(XNTM + osl * 512, 512),
                            af(XNTM + osl * 512, 512),
                            af(XLOAD + osl * 512, 512), ALU.add),
                           reads=("xl", "xntm"), writes=("xntm",))
                    op("sp", lambda e, tok0=tok0: e.dma_start(
                        out_t[tok0:tok0 + 128, :], af(XNTM, 2048)),
                       reads=("xntm",), writes=("outd",), dma=True)
        else:
            r.fence()
            for t in range(32):
                tok0 = t * 128
                op("sp", lambda e, tok0=tok0: e.dma_start(
                    ab(WIN0, 2048), E["s_ffn"][:, tok0:tok0 + 128], transpose=True),
                   reads=("sffn",), writes=("win0",), dma=True)
                op("sp", lambda e, tok0=tok0: e.dma_start(
                    af(XLOAD, 2048), E["x"][tok0:tok0 + 128, :]),
                   reads=(), writes=("xl",), dma=True)
                op("dve", lambda e: e.tensor_tensor(
                    af(XNTM, 2048), af(XLOAD, 2048), ab(WIN0, 2048), ALU.add),
                   reads=("xl", "win0"), writes=("xntm",))
                op("sp", lambda e, tok0=tok0: e.dma_start(
                    out_t[tok0:tok0 + 128, :], af(XNTM, 2048)),
                   reads=("xntm",), writes=("outd",), dma=True)

        if DBG2:
            r.fence()
            allb = tuple(r.bufs.keys())
            for i, (ar, off, cols, p0, pn) in enumerate(DBG2):
                dbg_t = nc.dram_tensor(f"dbh{i}", [pn, cols], F32,
                                       kind="ExternalOutput")
                src = (ab(off, cols, p0, pn) if ar == "ab"
                       else af(off, cols, p0, pn))
                op("gp", lambda e, d=dbg_t, s_=src: e.dma_start(d[:, :], s_),
                   reads=allb, writes=(f"dbh{i}",), dma=True)

        r.finalize()
        seq = {"pe": [], "act": [], "dve": [], "sp": [], "gp": []}
        for o in r.ops:
            seq[o.engine].append(o)
        semmap = {("pe", False): sems["cPE"], ("act", False): sems["cACT"],
                  ("dve", False): sems["cDVE"], ("sp", True): sems["dSP"],
                  ("gp", True): sems["dGP"]}

        def run_engine(e, name):
            waited = {}
            for o in seq[name]:
                for k, v in sorted(r.waits_for(o).items()):
                    if waited.get(k, -1) >= v:
                        continue
                    waited[k] = v
                    e.wait_ge(semmap[k], v)
                ins = o.emit(e)
                if o.needed:
                    ins.then_inc(semmap[(o.engine, o.is_dma)],
                                 16 if o.is_dma else 1)
            if name == "sp":
                e.wait_ge(sems["dSP"], 16 * r.final.get(("sp", True), 0))
            if name == "gp":
                e.wait_ge(sems["dGP"], 16 * r.final.get(("gp", True), 0))

        with nc.Block() as block:
            @block.tensor
            def _(e):
                run_engine(e, "pe")

            @block.scalar
            def _(e):
                run_engine(e, "act")

            @block.vector
            def _(e):
                run_engine(e, "dve")

            @block.sync
            def _(e):
                run_engine(e, "sp")

            @block.gpsimd
            def _(e):
                run_engine(e, "gp")

    return nc


_PROG = None


def _get_prog():
    global _PROG
    if _PROG is None:
        _PROG = build_program()
    return _PROG


def _common_inputs(inputs):
    f32 = np.float32
    ipw = np.asarray(inputs["L0_attn_in_proj_w"], f32)
    xpw = np.asarray(inputs["L0_attn_x_proj_w"], f32)
    opw = np.asarray(inputs["L0_attn_out_proj_w"], f32)
    Wr = np.asarray(inputs["L0_ffn_Wr"], f32)
    Wg = np.asarray(inputs["L0_ffn_Wg"], f32)
    Wu = np.asarray(inputs["L0_ffn_Wu"], f32)
    Wd = np.asarray(inputs["L0_ffn_Wd"], f32)
    cw = np.asarray(inputs["L0_attn_conv_w"], f32)
    cb = np.asarray(inputs["L0_attn_conv_b"], f32)

    def blocked(W):
        return np.ascontiguousarray(
            W.reshape(8, 16, 128, 16, 128).transpose(0, 3, 2, 1, 4)
            .reshape(16384, 2048))

    d = {
        "wr": np.ascontiguousarray(
            Wr.reshape(16, 128, 8).transpose(1, 0, 2).reshape(128, 128)),
        "w_inTz": np.ascontiguousarray(ipw[4096:].T),
        "w_inTxb": np.ascontiguousarray(
            ipw[:4096].reshape(32, 128, 16, 128).transpose(0, 3, 2, 1)
            .reshape(4096, 2048)),
        "w_xpT": np.ascontiguousarray(xpw.T),
        "w_outT": np.ascontiguousarray(opw.T),
        "w_dtT": np.ascontiguousarray(
            np.asarray(inputs["L0_attn_dt_proj_w"], f32).T),
        "wgb": blocked(Wg),
        "wub": blocked(Wu),
        "wdb": blocked(Wd),
        "convw": np.ascontiguousarray(
            cw.reshape(32, 128, 4).transpose(1, 0, 2).reshape(128, 128)),
        "convb": np.ascontiguousarray(cb.reshape(32, 128).T),
        "dtb": np.asarray(inputs["L0_attn_dt_proj_b"], f32).reshape(64, 1),
        "alog": np.asarray(inputs["L0_attn_A_log"], f32).reshape(64, 1),
        "dvec": np.asarray(inputs["L0_attn_D"], f32).reshape(64, 1),
        "negi64": (-np.eye(64)).astype(f32),
        "tri": np.where(np.arange(128)[:, None] <= np.arange(128)[None, :],
                        0.0, -1e30).astype(f32),
        "id128f": np.eye(128, dtype=f32),
        "id128b": np.eye(128, dtype=ml_dtypes.bfloat16),
        "id64f": np.eye(64, dtype=f32),
        "ones64": np.ones((64, 128), f32),
        "ones64b": np.ones((64, 128), ml_dtypes.bfloat16),
    }
    sel8 = np.zeros((8, 1024), ml_dtypes.bfloat16)
    for e in range(8):
        sel8[e, e * 128:(e + 1) * 128] = 1.0
    d["sel8"] = sel8
    d["epsc"] = np.full((128, 1), 1e-5, f32)
    return d


def kernel(**inputs):
    nc = _get_prog()
    common = _common_inputs(inputs)
    x = np.asarray(inputs["L0_attn_x"], np.float32)
    in_maps = []
    for c in range(NCORES):
        m = dict(common)
        m["x"] = np.ascontiguousarray(x[2 * c:2 * c + 2].reshape(TOK, DM))
        in_maps.append(m)
    res = run_bass_kernel_spmd(nc, in_maps, core_ids=list(range(NCORES)))
    global LAST_RES
    LAST_RES = res
    outs = [res.results[c]["out"].reshape(2, 2048, DM) for c in range(NCORES)]
    return np.concatenate(outs, axis=0)


# version 04
# speedup vs baseline: 1.0174x; 1.0174x over previous
import numpy as np
import ml_dtypes
import concourse.bass as bass
import concourse.mybir as mybir
from concourse.bass_utils import run_bass_kernel_spmd

F32 = mybir.dt.float32
BF16 = mybir.dt.bfloat16
AF = mybir.ActivationFunctionType
ALU = mybir.AluOpType
AX = mybir.AxisListType

NCORES = 8
TOK = 4096
DM = 2048
DI = 4096
NH = 64
CH = 128
OUT_T = 512
N_OUT = TOK // OUT_T

STAGE = "full"
DBG = []
DBG2 = []
LAST_RES = None

XNH = 0
GU = 32768
ACC = 49152
WS0 = 65536
WS1 = 67584
WU0 = 69632
TMPG = 71680
TMPGC = 72192
TMPU = 72704
WFM = 73216
BFMO = 74496
CFMO = 75008
M0O = 75520
BTO = 75648
AB_SZ = 75776

XTM = 0
XNT = 16384
XBR = 24576
SBF = 41088
DBC = 45184
EALL = 49280
XS = 57472
WIN0 = 61568
WIN1 = 63616
WXP0 = 65664
WXP1 = 69760
WTL0 = 73856
WTL1 = 74176

S0 = 0
XLOAD = 4096
XNTM = 6144
CT0 = 6144
CT1 = 6656
CUMA = 8192
FLT0 = 8704
TMPT = 9728
YDS = 10240
DTRAW = 10752
DT2 = 11264
CUMAT = 11776
SDOT = 11840
DTT = 11904
TMPDS = 11968
DS2 = 12096
DSTT = 12224
CDE2 = 12288
CDED = 12296
CDEB = 12360
WBC = 12424
RL = 12936
M1C = 13064
M2C = 13080
DLT = 13096
EXC = 13112
P1C = 13128
P2C = 13144
EQ1 = 13160
EQ2 = 13168
V1C = 13176
W8A = 13184
MX8 = 13312
AF_SZ = 13320
XL2AB = 36864


class Op:
    __slots__ = ("engine", "emit", "deps", "is_dma", "idx", "needed", "rank")

    def __init__(self, engine, emit, deps, is_dma, idx):
        self.engine, self.emit, self.deps, self.is_dma, self.idx = \
            engine, emit, deps, is_dma, idx
        self.needed = False
        self.rank = None


class Buf:
    __slots__ = ("name", "last_w", "readers")

    def __init__(self, name):
        self.name = name
        self.last_w = None
        self.readers = []


class Rec:
    def __init__(self):
        self.ops = []
        self.bufs = {}

    def buf(self, name):
        if name not in self.bufs:
            self.bufs[name] = Buf(name)
        return self.bufs[name]

    def op(self, engine, emit, reads=(), writes=(), dma=False):
        rb = [self.buf(x) for x in reads]
        wb = [self.buf(x) for x in writes]
        deps = set()
        for b in rb:
            if b.last_w is not None:
                deps.add(b.last_w)
        for b in wb:
            if b.last_w is not None:
                deps.add(b.last_w)
            deps.update(b.readers)
        o = Op(engine, emit, deps, dma, len(self.ops))
        self.ops.append(o)
        for b in rb:
            b.readers.append(o)
        for b in wb:
            b.last_w = o
            b.readers = []
        return o

    def fence(self):
        last = {}
        for o in self.ops:
            last[(o.engine, o.is_dma)] = o
        f = Op("dve", lambda e: e.engine_nop(), set(last.values()), False,
               len(self.ops))
        self.ops.append(f)
        for b in self.bufs.values():
            b.last_w = f
            b.readers = []

    SYNC_SAME = ("dve", "act")

    def finalize(self):
        for o in self.ops:
            if o.is_dma:
                o.needed = True
            for d in o.deps:
                if (d.is_dma or d.engine != o.engine
                        or d.engine in self.SYNC_SAME):
                    d.needed = True
        last = {}
        for o in self.ops:
            last[(o.engine, o.is_dma)] = o
        for o in last.values():
            o.needed = True
        cnt = {}
        for o in self.ops:
            if o.needed:
                k = (o.engine, o.is_dma)
                cnt[k] = cnt.get(k, 0) + 1
                o.rank = cnt[k]
        self.final = cnt

    def waits_for(self, o):
        best = {}
        for d in o.deps:
            if not (d.is_dma or d.engine != o.engine
                    or d.engine in self.SYNC_SAME):
                continue
            k = (d.engine, d.is_dma)
            v = d.rank * (16 if d.is_dma else 1)
            if v > best.get(k, -1):
                best[k] = v
        return best


def build_program():
    nc = bass.Bass("TRN2", target_bir_lowering=False)
    E = {}

    def din(name, shape, dtype=F32):
        E[name] = nc.dram_tensor(name, list(shape), dtype, kind="ExternalInput")

    def dint(name, shape, dtype):
        E[name] = nc.dram_tensor(name, list(shape), dtype, kind="Internal")

    din("x", (TOK, DM))
    din("wr", (128, 128))
    din("w_inTz", (DM, DI))
    din("w_inTxb", (DI, DM))
    din("w_xpT", (DI, 4416))
    din("w_outT", (DI, DM))
    din("w_dtT", (64, 64))
    din("wgb", (16384, 2048))
    din("wub", (16384, 2048))
    din("wdb", (16384, 2048))
    din("convw", (128, 128))
    din("convb", (128, 32))
    din("dtb", (64, 1))
    din("alog", (64, 1))
    din("dvec", (64, 1))
    din("negi64", (64, 64))
    din("tri", (128, 128), BF16)
    din("id128f", (128, 128))
    din("id128b", (128, 128), BF16)
    din("id64f", (64, 64))
    din("ones64", (64, 128))
    din("ones64b", (64, 128), BF16)
    din("sel8", (8, 1024), BF16)
    din("epsc", (128, 1))
    out_t = nc.dram_tensor("out", [TOK, DM], F32, kind="ExternalOutput")
    dint("s_wgb", (16384, 2048), BF16)
    dint("s_wub", (16384, 2048), BF16)
    dint("s_wdb", (16384, 2048), BF16)
    dint("s_inTz", (DM, DI), BF16)
    dint("s_inTxb", (DI, DM), BF16)
    dint("s_xpT", (DI, 4416), BF16)
    dint("s_outT", (DI, DM), BF16)
    dint("s_ffn", (DM, TOK), BF16)
    dint("s_flat", (64, 512), F32)

    r = Rec()
    op = r.op
    import contextlib
    with contextlib.ExitStack() as es:
        def sbt(name, shape, dtype):
            return es.enter_context(nc.sbuf_tensor(name, list(shape), dtype))

        AB = sbt("AB", [128, AB_SZ], BF16)
        AFt = sbt("AF", [128, AF_SZ], F32)
        c_wr = sbt("c_wr", [128, 128], F32)
        c_negi = sbt("c_negi", [64, 64], F32)
        c_tri = sbt("c_tri", [128, 128], BF16)
        c_idf = sbt("c_idf", [128, 128], F32)
        c_idb = sbt("c_idb", [128, 128], BF16)
        c_id64 = sbt("c_id64", [64, 64], F32)
        c_ones64 = sbt("c_ones64", [64, 128], F32)
        c_ones64b = sbt("c_ones64b", [64, 128], BF16)
        c_sel8 = sbt("c_sel8", [8, 1024], BF16)
        c_convw = sbt("c_convw", [128, 128], F32)
        c_convb = sbt("c_convb", [128, 32], F32)
        c_dtb = sbt("c_dtb", [64, 1], F32)
        c_alog = sbt("c_alog", [64, 1], F32)
        c_dvec = sbt("c_dvec", [64, 1], F32)
        negA = sbt("negA", [64, 1], F32)
        c_wdtT = sbt("c_wdtT", [64, 64], F32)
        c_eps = sbt("c_eps", [128, 1], F32)
        haloT = sbt("haloT", [128, 96], BF16)
        stats = sbt("stats", [128, 24], F32)

        pA = es.enter_context(nc.psum_tensor("pA", [128, 512], F32))
        pB = es.enter_context(nc.psum_tensor("pB", [128, 512], F32))
        pC = es.enter_context(nc.psum_tensor("pC", [128, 512], F32))
        pD = es.enter_context(nc.psum_tensor("pD", [128, 512], F32))
        pE = es.enter_context(nc.psum_tensor("pE", [128, 512], F32))
        pF = es.enter_context(nc.psum_tensor("pF", [128, 512], F32))
        pG = es.enter_context(nc.psum_tensor("pG", [128, 512], F32))
        pT = es.enter_context(nc.psum_tensor("pT", [128, 1024], BF16))
        PS = {"pA": pA, "pB": pB, "pC": pC, "pD": pD, "pE": pE, "pF": pF,
              "pG": pG, "pT": pT}

        sems = {}
        for n in ("cPE", "cACT", "cDVE", "dSP", "dGP"):
            sems[n] = es.enter_context(nc.semaphore(n))

        def ab(off, cols, p0=0, pn=128):
            return AB[p0:p0 + pn, off:off + cols]

        def af(off, cols, p0=0, pn=128):
            return AFt[p0:p0 + pn, off:off + cols]

        def cast_piece(dstname, srcname, a, b, bufname):
            op("gp", lambda e, a=a, b=b, d=E[dstname], s=E[srcname]:
               e.dma_start(d[a:b, :], s[a:b, :]),
               reads=(), writes=(bufname,), dma=True)

        for e_ in range(8):
            for t_, src, dst in (("g", "wgb", "s_wgb"), ("u", "wub", "s_wub"),
                                 ("d", "wdb", "s_wdb")):
                cast_piece(dst, src, e_ * 2048, (e_ + 1) * 2048, f"sw{t_}{e_}")
        if STAGE == "full":
            for nm, src, rows in (("s_inTxb", "w_inTxb", DI),
                                  ("s_inTz", "w_inTz", DM),
                                  ("s_xpT", "w_xpT", DI),
                                  ("s_outT", "w_outT", DI)):
                for i in range(0, rows, 2048):
                    cast_piece(nm, src, i, min(i + 2048, rows), nm)

        def load(dst_ap, srcname, bufname):
            op("sp", lambda e, d=dst_ap, s=E[srcname]: e.dma_start(d, s[:, :]),
               reads=(), writes=(bufname,), dma=True)

        for t, nm in ((c_wr, "wr"), (c_negi, "negi64"), (c_tri, "tri"),
                      (c_idf, "id128f"), (c_idb, "id128b"), (c_id64, "id64f"),
                      (c_ones64, "ones64"), (c_ones64b, "ones64b"),
                      (c_sel8, "sel8"), (c_convw, "convw"), (c_convb, "convb"),
                      (c_dtb, "dtb"), (c_alog, "alog"), (c_dvec, "dvec"),
                      (c_wdtT, "w_dtT"), (c_eps, "epsc")):
            load(t[:, :], nm, "c_" + nm)

        op("act", lambda e: e.activation(negA[:, :], c_alog[:, :], AF.Exp),
           reads=("c_alog",), writes=("negA",))
        op("dve", lambda e: e.tensor_scalar(negA[:, :], negA[:, :], -1.0, None,
                                            ALU.mult),
           reads=("negA",), writes=("negA",))

        def ln_tile(tok0, router_k, evac, sinkname="gu", par=0):
            sc = par * 12
            stn = f"stats{par}"
            xl = (af(XLOAD, 2048) if par == 0
                  else ab(XL2AB, 4096).bitcast(F32))
            xln = f"xl{par}"
            wr2 = (xln,) if par == 0 else (xln, sinkname)
            op("sp", lambda e, xl=xl: e.dma_start(xl,
                                                  E["x"][tok0:tok0 + 128, :]),
               reads=(), writes=wr2, dma=True)
            op("act", lambda e: e.activation(ab(GU, 2048), xl, AF.Square,
                                             accum_out=stats[:, sc + 1:sc + 2]),
               reads=(xln,), writes=(sinkname, stn))
            op("dve", lambda e: e.reduce_sum(stats[:, sc:sc + 1], xl,
                                             axis=AX.X),
               reads=(xln,), writes=(stn,))
            op("dve", lambda e: e.tensor_scalar(
                stats[:, sc + 2:sc + 3], stats[:, sc:sc + 1],
                1.0 / DM, None, ALU.mult),
               reads=(stn,), writes=(stn,))
            op("dve", lambda e: e.tensor_scalar(
                stats[:, sc + 3:sc + 4], stats[:, sc + 1:sc + 2],
                1.0 / DM, None, ALU.mult),
               reads=(stn,), writes=(stn,))
            op("dve", lambda e: e.tensor_tensor(
                stats[:, sc + 4:sc + 5], stats[:, sc + 2:sc + 3],
                stats[:, sc + 2:sc + 3], ALU.mult),
               reads=(stn,), writes=(stn,))
            op("dve", lambda e: e.tensor_tensor(
                stats[:, sc + 5:sc + 6], stats[:, sc + 3:sc + 4],
                stats[:, sc + 4:sc + 5], ALU.subtract),
               reads=(stn,), writes=(stn,))
            op("act", lambda e: e.activation(
                stats[:, sc + 6:sc + 7], stats[:, sc + 5:sc + 6],
                AF.Sqrt, bias=c_eps[:, 0:1]),
               reads=(stn, "c_epsc"), writes=(stn,))
            op("dve", lambda e: e.reciprocal(stats[:, sc + 7:sc + 8],
                                             stats[:, sc + 6:sc + 7]),
               reads=(stn,), writes=(stn,))
            op("dve", lambda e: e.tensor_scalar(
                af(XNTM, 2048), xl, stats[:, sc + 2:sc + 3],
                stats[:, sc + 7:sc + 8], ALU.subtract, ALU.mult),
               reads=(xln, stn), writes=("xntm",))
            for k in range(16):
                p, pn = (pA, "pA") if k % 2 == 0 else (pB, "pB")
                op("pe", lambda e, k=k, p=p: e.transpose(
                    p[:, 0:128], af(XNTM + 128 * k, 128), c_idf[:, :]),
                   reads=("xntm", "c_id128f"), writes=(pn,))
                if router_k:
                    op("dve", lambda e, p=p: e.tensor_copy(af(WBC, 128),
                                                           p[:, 0:128]),
                       reads=(pn,), writes=("wbc",))
                    op("pe", lambda e, k=k: e.matmul(
                        pG[0:8, 0:128], c_wr[:, 8 * k:8 * k + 8], af(WBC, 128),
                        start=(k == 0), stop=(k == 15)),
                       reads=("wbc", "c_wr"), writes=("pG",))
                evac(k, p, pn)

        for half in range(2):
            ht0 = half * 2048
            for j in range(16):
                def mo_evac(k, p, pn, j=j):
                    op("dve", lambda e, k=k, j=j, p=p: e.tensor_copy(
                        ab(XNH + k * 2048 + j * 128, 128), p[:, 0:128]),
                       reads=(pn,), writes=("xnh",))
                ln_tile(ht0 + j * 128, True, mo_evac, par=j % 2)
                op("dve", lambda e: e.tensor_copy(af(W8A, 128, 0, 8), pG[0:8, 0:128]),
                   reads=("pG",), writes=("lgf",))
                op("pe", lambda e: e.transpose(pG[0:128, 0:8],
                                               af(W8A, 128, 0, 8),
                                               c_idf[0:8, 0:8]),
                   reads=("lgf", "c_id128f"), writes=("pG",))
                op("dve", lambda e, j=j: e.tensor_copy(af(RL + 8 * j, 8),
                                                       pG[0:128, 0:8]),
                   reads=("pG",), writes=("rtr",))
                op("dve", lambda e, j=j: e.max(out=af(MX8, 8),
                                               in_=af(RL + 8 * j, 8)),
                   reads=("rtr",), writes=("rtr",))
                op("dve", lambda e, j=j: e.tensor_copy(af(M1C + j, 1),
                                                       af(MX8, 1)),
                   reads=("rtr",), writes=("rtr",))
                op("dve", lambda e, j=j: e.tensor_copy(af(M2C + j, 1),
                                                       af(MX8 + 1, 1)),
                   reads=("rtr",), writes=("rtr",))
                op("dve", lambda e, j=j: e.tensor_tensor(
                    af(DLT + j, 1), af(M2C + j, 1), af(M1C + j, 1),
                    ALU.subtract),
                   reads=("rtr",), writes=("rtr",))
            op("act", lambda e: e.activation(af(EXC, 16), af(DLT, 16), AF.Exp),
               reads=("rtr",), writes=("rtr",))
            op("dve", lambda e: e.tensor_scalar(af(EXC, 16), af(EXC, 16), 1.0,
                                                None, ALU.add),
               reads=("rtr",), writes=("rtr",))
            op("dve", lambda e: e.reciprocal(af(P1C, 16), af(EXC, 16)),
               reads=("rtr",), writes=("rtr",))
            op("dve", lambda e: e.tensor_scalar(af(P2C, 16), af(P1C, 16), -1.0,
                                                1.0, ALU.mult, ALU.add),
               reads=("rtr",), writes=("rtr",))
            for j in range(16):
                op("dve", lambda e, j=j: e.tensor_single_scalar(
                    af(EQ1, 8), af(RL + 8 * j, 8), af(M1C + j, 1),
                    ALU.is_equal),
                   reads=("rtr",), writes=("rtr8",))
                op("dve", lambda e, j=j: e.tensor_single_scalar(
                    af(EQ2, 8), af(RL + 8 * j, 8), af(M2C + j, 1),
                    ALU.is_equal),
                   reads=("rtr", "rtr8"), writes=("rtr8",))
                op("dve", lambda e, j=j: e.tensor_scalar(
                    af(V1C, 8), af(EQ1, 8), af(P1C + j, 1), None, ALU.mult),
                   reads=("rtr8", "rtr"), writes=("rtr8",))
                op("dve", lambda e, j=j: e.scalar_tensor_tensor(
                    af(W8A + 8 * j, 8), af(EQ2, 8), af(P2C + j, 1),
                    af(V1C, 8), ALU.mult, ALU.add),
                   reads=("rtr8", "rtr"), writes=("w8",))
                op("pe", lambda e, j=j: e.transpose(
                    pG[0:8, 0:128], af(W8A + 8 * j, 8), c_idf[:, :]),
                   reads=("w8", "c_id128f"), writes=("pG",))
                op("dve", lambda e, j=j: e.tensor_copy(
                    ab(WFM + j * 128, 128, 0, 8), pG[0:8, 0:128]),
                   reads=("pG",), writes=("wfm",))

            for tbD in range(2):
                t0 = tbD * 1024
                for e_ in range(8):
                    for t5 in range(2):
                        tt0 = t0 + t5 * 512
                        wb_off = WBC if t5 == 0 else CUMA
                        wbn = f"wbc{t5}"
                        op("pe", lambda e, e_=e_, tt0=tt0: e.matmul(
                            pG[:, 0:512],
                            c_sel8[:, e_ * 128:(e_ + 1) * 128],
                            ab(WFM + tt0, 512, 0, 8),
                            start=True, stop=True),
                           reads=("wfm", "c_sel8"), writes=("pG",))
                        op("dve", lambda e, wb_off=wb_off: e.tensor_copy(
                            af(wb_off, 512), pG[:, 0:512]),
                           reads=("pG",), writes=(wbn,))
                    for ic in range(16):
                        ws = WS0 if ic % 2 == 0 else WS1
                        wsn = f"wg{ic % 2}"
                        op("sp", lambda e, e_=e_, ic=ic, ws=ws: e.dma_start(
                            ab(ws, 2048),
                            E["s_wgb"][e_ * 2048 + ic * 128:
                                       e_ * 2048 + ic * 128 + 128, :]),
                           reads=(f"swg{e_}",), writes=(wsn,), dma=True)
                        op("sp", lambda e, e_=e_, ic=ic: e.dma_start(
                            ab(WU0, 2048),
                            E["s_wub"][e_ * 2048 + ic * 128:
                                       e_ * 2048 + ic * 128 + 128, :]),
                           reads=(f"swu{e_}",), writes=("wu0",), dma=True)
                        for k in range(16):
                            for t5 in range(2):
                                pg, pgn = (pA, "pA") if t5 == 0 else (pB, "pB")
                                op("pe", lambda e, k=k, ws=ws, t5=t5, pg=pg,
                                   t0=t0: e.matmul(
                                       pg[:, 0:512],
                                       ab(ws + k * 128, 128),
                                       ab(XNH + k * 2048 + t0 + t5 * 512, 512),
                                       start=(k == 0), stop=(k == 15)),
                                   reads=(wsn, "xnh"), writes=(pgn,))
                        for t5 in range(2):
                            pg, pgn = (pA, "pA") if t5 == 0 else (pB, "pB")
                            tg = TMPG if t5 == 0 else TMPU
                            tgn = f"tmpg{t5}"
                            op("act", lambda e, pg=pg, tg=tg: e.activation(
                                ab(tg, 512), pg[:, 0:512], AF.Silu),
                               reads=(pgn,), writes=(tgn,))
                            op("dve", lambda e, tg=tg: e.tensor_scalar(
                                ab(tg, 512), ab(tg, 512), 7.0, -7.0,
                                ALU.min, ALU.max),
                               reads=(tgn,), writes=(tgn,))
                        for k in range(16):
                            for t5 in range(2):
                                pu = pC if t5 == 0 else pD
                                pun = "pC" if t5 == 0 else "pD"
                                op("pe", lambda e, k=k, t5=t5, pu=pu, t0=t0:
                                   e.matmul(
                                       pu[:, 0:512],
                                       ab(WU0 + k * 128, 128),
                                       ab(XNH + k * 2048 + t0 + t5 * 512, 512),
                                       start=(k == 0), stop=(k == 15)),
                                   reads=("wu0", "xnh"), writes=(pun,))
                        for t5 in range(2):
                            pu = pC if t5 == 0 else pD
                            pun = "pC" if t5 == 0 else "pD"
                            tg = TMPG if t5 == 0 else TMPU
                            tgn = f"tmpg{t5}"
                            wb_off = WBC if t5 == 0 else CUMA
                            wbn = f"wbc{t5}"
                            op("dve", lambda e, pu=pu, wb_off=wb_off:
                               e.tensor_tensor(
                                   ab(TMPGC, 512), pu[:, 0:512],
                                   af(wb_off, 512), ALU.mult),
                               reads=(pun, wbn), writes=("tmpgc",))
                            op("dve", lambda e, ic=ic, t5=t5, tg=tg:
                               e.tensor_tensor(
                                   ab(GU + ic * 1024 + t5 * 512, 512),
                                   ab(tg, 512), ab(TMPGC, 512), ALU.mult),
                               reads=(tgn, "tmpgc"), writes=("gu",))
                    for oc in range(16):
                        ws = WS0 if oc % 2 == 0 else WS1
                        wsn = f"wg{oc % 2}"
                        op("sp", lambda e, e_=e_, oc=oc, ws=ws: e.dma_start(
                            ab(ws, 2048),
                            E["s_wdb"][e_ * 2048 + oc * 128:
                                       e_ * 2048 + oc * 128 + 128, :]),
                           reads=(f"swd{e_}",), writes=(wsn,), dma=True)
                        for ic in range(16):
                            for t5 in range(2):
                                pd_ = pE if t5 == 0 else pF
                                pdn = "pE" if t5 == 0 else "pF"
                                op("pe", lambda e, ic=ic, t5=t5, ws=ws,
                                   pd_=pd_: e.matmul(
                                       pd_[:, 0:512], ab(ws + ic * 128, 128),
                                       ab(GU + ic * 1024 + t5 * 512, 512),
                                       start=(ic == 0), stop=(ic == 15)),
                                   reads=(wsn, "gu"), writes=(pdn,))
                        for t5 in range(2):
                            pd_ = pE if t5 == 0 else pF
                            pdn = "pE" if t5 == 0 else "pF"
                            if e_ == 0:
                                op("dve", lambda e, oc=oc, t5=t5, pd_=pd_:
                                   e.tensor_copy(
                                       ab(ACC + oc * 1024 + t5 * 512, 512),
                                       pd_[:, 0:512]),
                                   reads=(pdn,), writes=("acc",))
                            else:
                                op("dve", lambda e, oc=oc, t5=t5, pd_=pd_:
                                   e.tensor_tensor(
                                       ab(ACC + oc * 1024 + t5 * 512, 512),
                                       ab(ACC + oc * 1024 + t5 * 512, 512),
                                       pd_[:, 0:512], ALU.add),
                                   reads=("acc", pdn), writes=("acc",))
                c0 = ht0 + t0
                dst = bass.AP(E["s_ffn"], c0,
                              [[4096, 128], [128 * 4096, 16], [1, 1024]])
                op("gp", lambda e, dst=dst: e.dma_start(dst, ab(ACC, 16384)),
                   reads=("acc",), writes=("sffn",), dma=True)

        if DBG:
            allb0 = tuple(r.bufs.keys())
            for i, (ar, off, cols, p0, pn) in enumerate(DBG):
                dbg_t = nc.dram_tensor(f"dbg{i}", [pn, cols], F32,
                                       kind="ExternalOutput")
                src = (ab(off, cols, p0, pn) if ar == "ab"
                       else af(off, cols, p0, pn))
                op("gp", lambda e, d=dbg_t, s_=src: e.dma_start(d[:, :], s_),
                   reads=allb0, writes=(f"dbg{i}",), dma=True)

        if STAGE == "full":
            r.fence()

            op("dve", lambda e: e.tensor_scalar(
                ab(EALL, 4096, 0, 64),
                bass.AP(c_id64, 0, [[64, 64], [1, 64], [0, 64]]),
                c_dvec[:, 0:1], None, ALU.mult),
               reads=("c_id64f", "c_dvec"), writes=("eall",))
            for q in range(8):
                op("pe", lambda e, q=q: e.matmul(
                    pA[:, 0:512], c_ones64b[:, :], ab(EALL + 512 * q, 512,
                                                      0, 64),
                    start=True, stop=True),
                   reads=("eall", "c_ones64b"), writes=("pA",))
                op("dve", lambda e, q=q: e.tensor_copy(ab(DBC + 512 * q, 512),
                                                       pA[:, 0:512]),
                   reads=("pA",), writes=("dbc",))

            for o_ in range(N_OUT):
                ot0 = o_ * OUT_T
                seq_start = (o_ % 4 == 0)
                for j in range(4):
                    def ma_evac(k, p, pn, j=j):
                        op("dve", lambda e, k=k, j=j, p=p: e.tensor_copy(
                            ab(XNT + k * 512 + j * 128, 128), p[:, 0:128]),
                           reads=(pn,), writes=("xnt",))
                    ln_tile(ot0 + j * 128, False, ma_evac,
                            sinkname="xbr", par=j % 2)

                if seq_start:
                    op("dve", lambda e: e.memset(haloT[:, :], 0.0),
                       reads=(), writes=("halo",))
                for ch in range(32):
                    op("dve", lambda e, ch=ch: e.tensor_copy(
                        ab(XBR + 516 * ch + 1, 3), haloT[:, 3 * ch:3 * ch + 3]),
                       reads=("halo",), writes=("xbr",))

                for oc in range(32):
                    ws = WIN0 if oc % 2 == 0 else WIN1
                    wsn = f"win{oc % 2}"
                    op("sp", lambda e, oc=oc, ws=ws: e.dma_start(
                        ab(ws, 2048),
                        E["s_inTxb"][oc * 128:(oc + 1) * 128, :]),
                       reads=("s_inTxb",), writes=(wsn,), dma=True)
                    p, pn = (pA, "pA") if oc % 2 == 0 else (pB, "pB")
                    for k in range(16):
                        op("pe", lambda e, k=k, ws=ws, p=p: e.matmul(
                            p[:, 0:512], ab(ws + k * 128, 128),
                            ab(XNT + k * 512, 512),
                            start=(k == 0), stop=(k == 15)),
                           reads=(wsn, "xnt"), writes=(pn,))
                    op("dve", lambda e, ch=oc, p=p: e.tensor_copy(
                        ab(XBR + 516 * ch + 4, 512), p[:, 0:512]),
                       reads=(pn,), writes=("xbr",))

                for ch in range(32):
                    op("dve", lambda e, ch=ch: e.tensor_copy(
                        haloT[:, 3 * ch:3 * ch + 3],
                        ab(XBR + 516 * ch + 513, 3)),
                       reads=("xbr",), writes=("halo",))
                    op("dve", lambda e, ch=ch: e.tensor_scalar(
                        af(CT0, 512), ab(XBR + 516 * ch + 1, 512),
                        c_convw[:, 4 * ch:4 * ch + 1], None, ALU.mult),
                       reads=("xbr", "c_convw"), writes=("xntm",))
                    for k in range(1, 4):
                        src, dst = (CT0, CT1) if k % 2 == 1 else (CT1, CT0)
                        op("dve", lambda e, ch=ch, k=k, src=src, dst=dst:
                           e.scalar_tensor_tensor(
                               af(dst, 512), ab(XBR + 516 * ch + 1 + k, 512),
                               c_convw[:, 4 * ch + k:4 * ch + k + 1],
                               af(src, 512), ALU.mult, ALU.add),
                           reads=("xbr", "c_convw", "xntm"), writes=("xntm",))
                    op("act", lambda e, ch=ch: e.activation(
                        ab(XBR + 516 * ch + 4, 512), af(CT1, 512), AF.Silu,
                        bias=c_convb[:, ch:ch + 1]),
                       reads=("xntm", "c_convb"), writes=("xbr",))

                for osl in range(8):
                    for kg in range(4):
                        ws = WXP0 if kg % 2 == 0 else WXP1
                        wsn = f"wxp{kg % 2}"
                        src = bass.AP(
                            E["s_xpT"], kg * 1024 * 4416 + osl * 512,
                            [[4416, 128], [128 * 4416, 8], [1, 512]])
                        op("sp", lambda e, ws=ws, src=src: e.dma_start(
                            ab(ws, 4096), src),
                           reads=("s_xpT",), writes=(wsn,), dma=True)
                        for k8 in range(8):
                            k = kg * 8 + k8
                            for tl in range(4):
                                pp = (pC, pD, pE, pF)[tl]
                                ppn = ("pC", "pD", "pE", "pF")[tl]
                                op("pe", lambda e, k=k, k8=k8, tl=tl, ws=ws,
                                   pp=pp: e.matmul(
                                       pp[:, 0:512],
                                       ab(XBR + 516 * k + 4 + tl * 128, 128),
                                       ab(ws + k8 * 512, 512),
                                       start=(k == 0), stop=(k == 31)),
                                   reads=(wsn, "xbr"), writes=(ppn,))
                    for tl in range(4):
                        pp = (pC, pD, pE, pF)[tl]
                        ppn = ("pC", "pD", "pE", "pF")[tl]
                        op("dve", lambda e, tl=tl, osl=osl, pp=pp:
                           e.tensor_copy(
                               ab(XTM + tl * 4096 + osl * 512, 512),
                               pp[:, 0:512]),
                           reads=(ppn,), writes=("xtm",))

                kgroups = [list(range(g, min(g + 3, 32)))
                           for g in range(0, 32, 3)]
                for gi, grp in enumerate(kgroups):
                    ws = WTL0 if gi % 2 == 0 else WTL1
                    wsn = f"wtl{gi % 2}"
                    k0, nk = grp[0], len(grp)
                    src = bass.AP(E["s_xpT"], k0 * 128 * 4416 + 4096,
                                  [[4416, 128], [128 * 4416, nk], [1, 320]])
                    op("sp", lambda e, ws=ws, src=src, nk=nk: e.dma_start(
                        ab(ws, nk * 320), src),
                       reads=("s_xpT",), writes=(wsn,), dma=True)
                    for ji, k in enumerate(grp):
                        wo = ws + ji * 320
                        op("pe", lambda e, k=k, wo=wo: e.matmul(
                            pA[0:64, 0:512], ab(wo, 64),
                            ab(XBR + 516 * k + 4, 512),
                            start=(k == 0), stop=(k == 31)),
                           reads=(wsn, "xbr"), writes=("pA",))
                        op("pe", lambda e, k=k, wo=wo: e.matmul(
                            pB[:, 0:512], ab(wo + 64, 128),
                            ab(XBR + 516 * k + 4, 512),
                            start=(k == 0), stop=(k == 31)),
                           reads=(wsn, "xbr"), writes=("pB",))
                        op("pe", lambda e, k=k, wo=wo: e.matmul(
                            pG[:, 0:512], ab(wo + 192, 128),
                            ab(XBR + 516 * k + 4, 512),
                            start=(k == 0), stop=(k == 31)),
                           reads=(wsn, "xbr"), writes=("pG",))
                op("dve", lambda e: e.tensor_copy(af(DTRAW, 512, 0, 64),
                                                  pA[0:64, 0:512]),
                   reads=("pA",), writes=("dtraw",))
                op("dve", lambda e: e.tensor_copy(ab(BFMO, 512), pB[:, 0:512]),
                   reads=("pB",), writes=("bfm",))
                op("dve", lambda e: e.tensor_copy(ab(CFMO, 512), pG[:, 0:512]),
                   reads=("pG",), writes=("cfm",))

                op("pe", lambda e: e.matmul(pC[0:64, 0:512], c_wdtT[:, :],
                                            af(DTRAW, 512, 0, 64),
                                            start=True, stop=True),
                   reads=("dtraw", "c_w_dtT"), writes=("pC",))
                op("act", lambda e: e.activation(af(TMPT, 512, 0, 64),
                                                 pC[0:64, 0:512], AF.Abs,
                                                 bias=c_dtb[:, 0:1]),
                   reads=("pC", "c_dtb"), writes=("tmpt",))
                op("act", lambda e: e.activation(af(YDS, 512, 0, 64),
                                                 af(TMPT, 512, 0, 64), AF.Exp,
                                                 scale=-1.0),
                   reads=("tmpt",), writes=("yds",))
                op("act", lambda e: e.activation(af(TMPT, 512, 0, 64),
                                                 af(YDS, 512, 0, 64), AF.Ln,
                                                 bias=1.0),
                   reads=("yds",), writes=("tmpt",))
                op("act", lambda e: e.activation(af(YDS, 512, 0, 64),
                                                 pC[0:64, 0:512], AF.Relu,
                                                 bias=c_dtb[:, 0:1]),
                   reads=("pC", "c_dtb"), writes=("yds",))
                op("dve", lambda e: e.tensor_tensor(af(DT2, 512, 0, 64),
                                                    af(TMPT, 512, 0, 64),
                                                    af(YDS, 512, 0, 64),
                                                    ALU.add),
                   reads=("tmpt", "yds"), writes=("dt2",))
                op("dve", lambda e: e.tensor_scalar(af(DTRAW, 512, 0, 64),
                                                    af(DT2, 512, 0, 64),
                                                    negA[:, 0:1], None,
                                                    ALU.mult),
                   reads=("dt2", "negA"), writes=("dtraw",))
                for c in range(4):
                    op("dve", lambda e, c=c: e.tensor_tensor_scan(
                        af(CUMA + 128 * c, 128, 0, 64),
                        af(DTRAW + 128 * c, 128, 0, 64),
                        af(DTRAW + 128 * c, 128, 0, 64), 0.0,
                        ALU.add, ALU.bypass),
                       reads=("dtraw",), writes=("cuma",))
                op("gp", lambda e: e.dma_start(E["s_flat"][:, :],
                                               af(CUMA, 512, 0, 64)),
                   reads=("cuma",), writes=("sflat",), dma=True)

                if seq_start:
                    op("dve", lambda e: e.memset(af(S0, 4096), 0.0),
                       reads=(), writes=("S",))
                    op("dve", lambda e: e.memset(ab(SBF, 4096), 0.0),
                       reads=(), writes=("sbf",))

                op("act", lambda e: e.activation(ab(SBF, 2048),
                                                 af(S0, 2048), AF.Identity),
                   reads=("S",), writes=("sbf",))
                op("act", lambda e: e.activation(ab(SBF + 2048, 2048),
                                                 af(S0 + 2048, 2048),
                                                 AF.Identity),
                   reads=("S",), writes=("sbf",))

                for c in range(4):
                    for js in range(16):
                        fb = FLT0 + (js % 2) * 512
                        fbn = f"flt{js % 2}"
                        src = bass.AP(E["s_flat"], 4 * js * 512 + 128 * c,
                                      [[0, 128], [512, 4], [1, 128]])
                        op("gp", lambda e, src=src, fb=fb: e.dma_start(
                            af(fb, 512), src),
                           reads=("sflat",), writes=(fbn,), dma=True)
                        p, pn = (pA, "pA") if js % 2 == 0 else (pB, "pB")
                        rhs_negi = bass.AP(c_negi, 4 * js,
                                           [[64, 64], [1, 4], [0, 128]])
                        op("pe", lambda e, c=c, p=p, rn=rhs_negi: e.matmul(
                            p[:, 0:512], af(CUMA + 128 * c, 128, 0, 64), rn,
                            start=True, stop=True),
                           reads=("cuma", "c_negi64"), writes=(pn,))
                        op("dve", lambda e, p=p: e.tensor_tensor(
                            af(TMPT, 512), p[:, 0:512],
                            bass.AP(c_tri, 0, [[128, 128], [0, 4], [1, 128]]),
                            ALU.add),
                           reads=(pn, "c_tri"), writes=("tmpt",))
                        op("dve", lambda e, fb=fb: e.tensor_tensor(
                            af(TMPT, 512), af(TMPT, 512), af(fb, 512),
                            ALU.add),
                           reads=("tmpt", fbn), writes=("tmpt",))
                        op("act", lambda e, js=js: e.activation(
                            ab(EALL + 512 * js, 512), af(TMPT, 512), AF.Exp),
                           reads=("tmpt",), writes=("eall",))
                    op("pe", lambda e, c=c: e.transpose(
                        pC[:, 0:64], af(CUMA + 128 * c, 128, 0, 64),
                        c_id64[:, :]),
                       reads=("cuma", "c_id64f"), writes=("pC",))
                    op("dve", lambda e: e.tensor_copy(af(CUMAT, 64),
                                                      pC[:, 0:64]),
                       reads=("pC",), writes=("cumat",))
                    op("act", lambda e: e.activation(af(SDOT, 64),
                                                     af(CUMAT, 64), AF.Exp),
                       reads=("cumat",), writes=("sdot",))
                    op("pe", lambda e, c=c: e.transpose(
                        pC[:, 64:128], af(DT2 + 128 * c, 128, 0, 64),
                        c_id64[:, :]),
                       reads=("dt2", "c_id64f"), writes=("pC",))
                    op("dve", lambda e: e.tensor_copy(af(DTT, 64),
                                                      pC[:, 64:128]),
                       reads=("pC",), writes=("dtt",))
                    op("dve", lambda e, c=c: e.tensor_scalar(
                        af(TMPDS, 128, 0, 64), af(CUMA + 128 * c, 128, 0, 64),
                        -1.0, af(CUMA + 128 * c + 127, 1, 0, 64),
                        ALU.mult, ALU.add),
                       reads=("cuma",), writes=("tmpds",))
                    op("act", lambda e: e.activation(af(DS2, 128, 0, 64),
                                                     af(TMPDS, 128, 0, 64),
                                                     AF.Exp),
                       reads=("tmpds",), writes=("ds2",))
                    op("pe", lambda e: e.transpose(pD[:, 0:64],
                                                   af(DS2, 128, 0, 64),
                                                   c_id64[:, :]),
                       reads=("ds2", "c_id64f"), writes=("pD",))
                    op("dve", lambda e: e.tensor_copy(af(DSTT, 64),
                                                      pD[:, 0:64]),
                       reads=("pD",), writes=("dstt",))
                    op("act", lambda e, c=c: e.activation(
                        af(CDE2, 1, 0, 64), af(CUMA + 128 * c + 127, 1, 0, 64),
                        AF.Exp),
                       reads=("cuma",), writes=("cde2",))
                    op("dve", lambda e: e.tensor_scalar(
                        af(CDED, 64, 0, 64), c_id64[:, :], af(CDE2, 1, 0, 64),
                        None, ALU.mult),
                       reads=("cde2", "c_id64f"), writes=("cded",))
                    op("pe", lambda e: e.matmul(pD[:, 64:128], c_ones64[:, :],
                                                af(CDED, 64, 0, 64),
                                                start=True, stop=True),
                       reads=("cded", "c_ones64"), writes=("pD",))
                    op("dve", lambda e: e.tensor_copy(af(CDEB, 64),
                                                      pD[:, 64:128]),
                       reads=("pD",), writes=("cdeb",))
                    op("pe", lambda e, c=c: e.matmul(
                        pD[:, 128:256], ab(BFMO + 128 * c, 128),
                        ab(CFMO + 128 * c, 128), start=True, stop=True),
                       reads=("bfm", "cfm"), writes=("pD",))
                    op("dve", lambda e: e.tensor_copy(ab(M0O, 128), pD[:, 128:256]),
                       reads=("pD",), writes=("m0",))
                    op("pe", lambda e, c=c: e.transpose(
                        pT[:, 0:128], ab(BFMO + 128 * c, 128),
                        c_idb[:, :]),
                       reads=("bfm", "c_id128b"), writes=("pT",))
                    op("dve", lambda e: e.tensor_copy(ab(BTO, 128), pT[:, 0:128]),
                       reads=("pT",), writes=("bt",))

                    for g8 in range(8):
                        pyd, pydn = (pE, "pE") if g8 % 2 == 0 else (pF, "pF")
                        hs = [8 * g8 + i for i in range(8)]
                        for h in hs:
                            op("dve", lambda e, h=h: e.tensor_tensor(
                                ab(EALL + 128 * h, 128), ab(EALL + 128 * h, 128),
                                ab(M0O, 128), ALU.mult),
                               reads=("eall", "m0"), writes=("eall",))
                            op("dve", lambda e, h=h, c=c: e.tensor_scalar(
                                ab(XS + 64 * h, 64),
                                ab(XTM + c * 4096 + 64 * h, 64),
                                af(DTT + h, 1), None, ALU.mult),
                               reads=("xtm", "dtt"), writes=("xs",))
                        for h in hs:
                            op("pe", lambda e, h=h, pyd=pyd: e.matmul(
                                pyd[:, 64 * (h % 8):64 * (h % 8) + 64],
                                ab(EALL + 128 * h, 128), ab(XS + 64 * h, 64),
                                start=True, stop=True),
                               reads=("eall", "xs"), writes=(pydn,))
                        for h in hs:
                            op("act", lambda e, h=h, pyd=pyd: e.activation(
                                af(YDS + 64 * (h % 8), 64),
                                pyd[:, 64 * (h % 8):64 * (h % 8) + 64],
                                AF.Identity),
                               reads=(pydn,), writes=("yds",))
                        for h in hs:
                            op("dve", lambda e, h=h: e.tensor_scalar(
                                ab(XS + 64 * h, 64), ab(XS + 64 * h, 64),
                                af(DSTT + h, 1), None, ALU.mult),
                               reads=("xs", "dstt"), writes=("xs",))
                        for g4 in range(2):
                            hq = hs[4 * g4:4 * g4 + 4]
                            op("pe", lambda e, c=c, g4=g4, hq=hq: e.matmul(
                                pG[:, 256 * g4:256 * g4 + 256],
                                ab(CFMO + 128 * c, 128),
                                ab(SBF + 64 * hq[0], 256),
                                start=True, stop=True),
                               reads=("cfm", "sbf"), writes=("pG",))
                            for h in hq:
                                op("dve", lambda e, h=h, c=c, g4=g4:
                                   e.scalar_tensor_tensor(
                                       ab(XBR + c * 4096 + 64 * h, 64),
                                       pG[:, 256 * g4 + 64 * (h % 4):
                                          256 * g4 + 64 * (h % 4) + 64],
                                       af(SDOT + h, 1),
                                       af(YDS + 64 * (h % 8), 64),
                                       ALU.mult, ALU.add),
                                   reads=("pG", "sdot", "yds"),
                                   writes=("xbr",))
                            pm = pC if g4 == 0 else pD
                            pmn = "pC" if g4 == 0 else "pD"
                            op("pe", lambda e, hq=hq, pm=pm: e.matmul(
                                pm[:, 256:512], ab(BTO, 128),
                                ab(XS + 64 * hq[0], 256),
                                start=True, stop=True),
                               reads=("bt", "xs"), writes=(pmn,))
                            for h in hq:
                                op("dve", lambda e, h=h, pm=pm:
                                   e.scalar_tensor_tensor(
                                       af(S0 + 64 * h, 64), af(S0 + 64 * h, 64),
                                       af(CDEB + h, 1),
                                       pm[:, 256 + 64 * (h % 4):
                                          256 + 64 * (h % 4) + 64],
                                       ALU.mult, ALU.add),
                                   reads=("S", "cdeb", pmn), writes=("S",))
                    op("act", lambda e: e.activation(ab(SBF, 2048),
                                                     af(S0, 2048), AF.Identity),
                       reads=("S",), writes=("sbf",))
                    op("act", lambda e: e.activation(ab(SBF + 2048, 2048),
                                                     af(S0 + 2048, 2048),
                                                     AF.Identity),
                       reads=("S",), writes=("sbf",))

                for j in range(4):
                    tok0 = ot0 + j * 128
                    for osl in range(8):
                        pz, pzn = (pA, "pA") if osl % 2 == 0 else (pB, "pB")
                        for kg in range(2):
                            ws = WXP0 if kg % 2 == 0 else WXP1
                            wsn = f"wxp{kg % 2}"
                            src = bass.AP(
                                E["s_inTz"], kg * 1024 * 4096 + osl * 512,
                                [[4096, 128], [128 * 4096, 8], [1, 512]])
                            op("sp", lambda e, ws=ws, src=src: e.dma_start(
                                ab(ws, 4096), src),
                               reads=("s_inTz",), writes=(wsn,), dma=True)
                            for k8 in range(8):
                                k = kg * 8 + k8
                                op("pe", lambda e, k=k, k8=k8, j=j, ws=ws,
                                   pz=pz: e.matmul(
                                       pz[:, 0:512],
                                       ab(XNT + k * 512 + j * 128, 128),
                                       ab(ws + k8 * 512, 512),
                                       start=(k == 0), stop=(k == 15)),
                                   reads=(wsn, "xnt"), writes=(pzn,))
                        op("act", lambda e, osl=osl, pz=pz: e.activation(
                            ab(XS + osl * 512, 512), pz[:, 0:512], AF.Silu),
                           reads=(pzn,), writes=("xs",))
                    op("dve", lambda e, j=j: e.tensor_tensor(
                        ab(EALL, 4096), ab(XTM + j * 4096, 4096),
                        ab(DBC, 4096), ALU.mult),
                       reads=("xtm", "dbc"), writes=("eall",))
                    op("dve", lambda e, j=j: e.tensor_tensor(
                        ab(EALL, 4096), ab(EALL, 4096),
                        ab(XBR + j * 4096, 4096), ALU.add),
                       reads=("eall", "xbr"), writes=("eall",))
                    op("dve", lambda e: e.tensor_tensor(
                        ab(EALL, 4096), ab(EALL, 4096), ab(XS, 4096),
                        ALU.mult),
                       reads=("eall", "xs"), writes=("eall",))
                    for k in range(32):
                        sl = k % 8
                        op("pe", lambda e, k=k, sl=sl: e.transpose(
                            pT[:, 128 * sl:128 * sl + 128],
                            ab(EALL + 128 * k, 128), c_idb[:, :]),
                           reads=("eall", "c_id128b"), writes=("pT",))
                        op("dve", lambda e, k=k, sl=sl: e.tensor_copy(
                            ab(EALL + 4096 + 128 * k, 128),
                            pT[:, 128 * sl:128 * sl + 128]),
                           reads=("pT",), writes=("eall",))
                    op("sp", lambda e, tok0=tok0: e.dma_start(
                        ab(WIN0, 2048), E["s_ffn"][:, tok0:tok0 + 128],
                        transpose=True),
                       reads=("sffn",), writes=("win0",), dma=True)
                    op("sp", lambda e, tok0=tok0: e.dma_start(
                        af(XLOAD, 2048), E["x"][tok0:tok0 + 128, :]),
                       reads=(), writes=("xl0",), dma=True)
                    for osl in range(4):
                        po, pon = (pC, "pC") if osl % 2 == 0 else (pD, "pD")
                        for kg in range(4):
                            ws = WXP0 if kg % 2 == 0 else WXP1
                            wsn = f"wxp{kg % 2}"
                            src = bass.AP(
                                E["s_outT"], kg * 1024 * 2048 + osl * 512,
                                [[2048, 128], [128 * 2048, 8], [1, 512]])
                            op("sp", lambda e, ws=ws, src=src: e.dma_start(
                                ab(ws, 4096), src),
                               reads=("s_outT",), writes=(wsn,), dma=True)
                            for k8 in range(8):
                                k = kg * 8 + k8
                                op("pe", lambda e, k=k, k8=k8, ws=ws, po=po:
                                   e.matmul(
                                       po[:, 0:512],
                                       ab(EALL + 4096 + 128 * k, 128),
                                       ab(ws + k8 * 512, 512),
                                       start=(k == 0), stop=(k == 31)),
                                   reads=(wsn, "eall"), writes=(pon,))
                        op("dve", lambda e, osl=osl, po=po:
                           e.scalar_tensor_tensor(
                               af(XNTM + osl * 512, 512), po[:, 0:512], 1.0,
                               ab(WIN0 + osl * 512, 512),
                               ALU.mult, ALU.add),
                           reads=(pon, "win0"), writes=("xntm",))
                        op("dve", lambda e, osl=osl: e.tensor_tensor(
                            af(XNTM + osl * 512, 512),
                            af(XNTM + osl * 512, 512),
                            af(XLOAD + osl * 512, 512), ALU.add),
                           reads=("xl0", "xntm"), writes=("xntm",))
                    op("sp", lambda e, tok0=tok0: e.dma_start(
                        out_t[tok0:tok0 + 128, :], af(XNTM, 2048)),
                       reads=("xntm",), writes=("outd",), dma=True)
        else:
            r.fence()
            for t in range(32):
                tok0 = t * 128
                op("sp", lambda e, tok0=tok0: e.dma_start(
                    ab(WIN0, 2048), E["s_ffn"][:, tok0:tok0 + 128], transpose=True),
                   reads=("sffn",), writes=("win0",), dma=True)
                op("sp", lambda e, tok0=tok0: e.dma_start(
                    af(XLOAD, 2048), E["x"][tok0:tok0 + 128, :]),
                   reads=(), writes=("xl",), dma=True)
                op("dve", lambda e: e.tensor_tensor(
                    af(XNTM, 2048), af(XLOAD, 2048), ab(WIN0, 2048), ALU.add),
                   reads=("xl0", "win0"), writes=("xntm",))
                op("sp", lambda e, tok0=tok0: e.dma_start(
                    out_t[tok0:tok0 + 128, :], af(XNTM, 2048)),
                   reads=("xntm",), writes=("outd",), dma=True)

        if DBG2:
            r.fence()
            allb = tuple(r.bufs.keys())
            for i, (ar, off, cols, p0, pn) in enumerate(DBG2):
                dbg_t = nc.dram_tensor(f"dbh{i}", [pn, cols], F32,
                                       kind="ExternalOutput")
                src = (ab(off, cols, p0, pn) if ar == "ab"
                       else af(off, cols, p0, pn))
                op("gp", lambda e, d=dbg_t, s_=src: e.dma_start(d[:, :], s_),
                   reads=allb, writes=(f"dbh{i}",), dma=True)

        r.finalize()
        seq = {"pe": [], "act": [], "dve": [], "sp": [], "gp": []}
        for o in r.ops:
            seq[o.engine].append(o)
        semmap = {("pe", False): sems["cPE"], ("act", False): sems["cACT"],
                  ("dve", False): sems["cDVE"], ("sp", True): sems["dSP"],
                  ("gp", True): sems["dGP"]}

        def run_engine(e, name):
            waited = {}
            for o in seq[name]:
                for k, v in sorted(r.waits_for(o).items()):
                    if waited.get(k, -1) >= v:
                        continue
                    waited[k] = v
                    e.wait_ge(semmap[k], v)
                ins = o.emit(e)
                if o.needed:
                    ins.then_inc(semmap[(o.engine, o.is_dma)],
                                 16 if o.is_dma else 1)
            if name == "sp":
                e.wait_ge(sems["dSP"], 16 * r.final.get(("sp", True), 0))
            if name == "gp":
                e.wait_ge(sems["dGP"], 16 * r.final.get(("gp", True), 0))

        with nc.Block() as block:
            @block.tensor
            def _(e):
                run_engine(e, "pe")

            @block.scalar
            def _(e):
                run_engine(e, "act")

            @block.vector
            def _(e):
                run_engine(e, "dve")

            @block.sync
            def _(e):
                run_engine(e, "sp")

            @block.gpsimd
            def _(e):
                run_engine(e, "gp")

    return nc


_PROG = None


def _get_prog():
    global _PROG
    if _PROG is None:
        _PROG = build_program()
    return _PROG


def _common_inputs(inputs):
    f32 = np.float32
    ipw = np.asarray(inputs["L0_attn_in_proj_w"], f32)
    xpw = np.asarray(inputs["L0_attn_x_proj_w"], f32)
    opw = np.asarray(inputs["L0_attn_out_proj_w"], f32)
    Wr = np.asarray(inputs["L0_ffn_Wr"], f32)
    Wg = np.asarray(inputs["L0_ffn_Wg"], f32)
    Wu = np.asarray(inputs["L0_ffn_Wu"], f32)
    Wd = np.asarray(inputs["L0_ffn_Wd"], f32)
    cw = np.asarray(inputs["L0_attn_conv_w"], f32)
    cb = np.asarray(inputs["L0_attn_conv_b"], f32)

    def blocked(W):
        return np.ascontiguousarray(
            W.reshape(8, 16, 128, 16, 128).transpose(0, 3, 2, 1, 4)
            .reshape(16384, 2048))

    d = {
        "wr": np.ascontiguousarray(
            Wr.reshape(16, 128, 8).transpose(1, 0, 2).reshape(128, 128)),
        "w_inTz": np.ascontiguousarray(ipw[4096:].T),
        "w_inTxb": np.ascontiguousarray(
            ipw[:4096].reshape(32, 128, 16, 128).transpose(0, 3, 2, 1)
            .reshape(4096, 2048)),
        "w_xpT": np.ascontiguousarray(xpw.T),
        "w_outT": np.ascontiguousarray(opw.T),
        "w_dtT": np.ascontiguousarray(
            np.asarray(inputs["L0_attn_dt_proj_w"], f32).T),
        "wgb": blocked(Wg),
        "wub": blocked(Wu),
        "wdb": blocked(Wd),
        "convw": np.ascontiguousarray(
            cw.reshape(32, 128, 4).transpose(1, 0, 2).reshape(128, 128)),
        "convb": np.ascontiguousarray(cb.reshape(32, 128).T),
        "dtb": np.asarray(inputs["L0_attn_dt_proj_b"], f32).reshape(64, 1),
        "alog": np.asarray(inputs["L0_attn_A_log"], f32).reshape(64, 1),
        "dvec": np.asarray(inputs["L0_attn_D"], f32).reshape(64, 1),
        "negi64": (-np.eye(64)).astype(f32),
        "tri": np.where(np.arange(128)[:, None] <= np.arange(128)[None, :],
                        0.0, -1e30).astype(ml_dtypes.bfloat16),
        "id128f": np.eye(128, dtype=f32),
        "id128b": np.eye(128, dtype=ml_dtypes.bfloat16),
        "id64f": np.eye(64, dtype=f32),
        "ones64": np.ones((64, 128), f32),
        "ones64b": np.ones((64, 128), ml_dtypes.bfloat16),
    }
    sel8 = np.zeros((8, 1024), ml_dtypes.bfloat16)
    for e in range(8):
        sel8[e, e * 128:(e + 1) * 128] = 1.0
    d["sel8"] = sel8
    d["epsc"] = np.full((128, 1), 1e-5, f32)
    return d


def kernel(**inputs):
    nc = _get_prog()
    common = _common_inputs(inputs)
    x = np.asarray(inputs["L0_attn_x"], np.float32)
    in_maps = []
    for c in range(NCORES):
        m = dict(common)
        m["x"] = np.ascontiguousarray(x[2 * c:2 * c + 2].reshape(TOK, DM))
        in_maps.append(m)
    res = run_bass_kernel_spmd(nc, in_maps, core_ids=list(range(NCORES)))
    global LAST_RES
    LAST_RES = res
    outs = [res.results[c]["out"].reshape(2, 2048, DM) for c in range(NCORES)]
    return np.concatenate(outs, axis=0)


# version 06 (final)
# speedup vs baseline: 1.1057x; 1.0868x over previous
import numpy as np
import ml_dtypes
import concourse.bass as bass
import concourse.mybir as mybir
from concourse.bass_utils import run_bass_kernel_spmd

F32 = mybir.dt.float32
BF16 = mybir.dt.bfloat16
AF = mybir.ActivationFunctionType
ALU = mybir.AluOpType
AX = mybir.AxisListType

NCORES = 8
TOK = 4096
DM = 2048
DI = 4096
NH = 64
CH = 128
OUT_T = 512
N_OUT = TOK // OUT_T

STAGE = "full"
DBG = []
DBG2 = []
LAST_RES = None

XNH = 0
GU = 32768
ACC = 49152
WS0 = 65536
WS1 = 67584
WU0 = 69632
TMPG = 71680
TMPGC = 72192
TMPU = 72704
WFM = 73216
BFMO = 74496
CFMO = 75008
M0O = 75520
BTO = 75648
AB_SZ = 75776

XTM = 0
XNT = 16384
XBR = 24576
SBF = 41088
DBC = 45184
EALL = 49280
XS = 57472
WIN0 = 61568
WIN1 = 63616
WXP0 = 65664
WXP1 = 69760
WTL0 = 73856
WTL1 = 74816
DTTB = 73216
DSTB = 73280

S0 = 0
XLOAD = 4096
XNTM = 6144
CT0 = 6144
CT1 = 6656
CUMA = 8192
FLT0 = 8704
TMPT = 9728
YDS = 10240
DTRAW = 10752
DT2 = 11264
CUMAT = 11776
SDOT = 11840
DTT = 11904
TMPDS = 11968
DS2 = 12096
DSTT = 12224
CDE2 = 12288
CDED = 12296
CDEB = 12360
WBC = 12424
RL = 12936
M1C = 13064
M2C = 13080
DLT = 13096
EXC = 13112
P1C = 13128
P2C = 13144
EQ1 = 13160
EQ2 = 13168
V1C = 13176
W8A = 13184
MX8 = 13312
AF_SZ = 13320
XL2AB = 36864


class Op:
    __slots__ = ("engine", "emit", "deps", "is_dma", "idx", "needed", "rank")

    def __init__(self, engine, emit, deps, is_dma, idx):
        self.engine, self.emit, self.deps, self.is_dma, self.idx = \
            engine, emit, deps, is_dma, idx
        self.needed = False
        self.rank = None


class Buf:
    __slots__ = ("name", "last_w", "readers")

    def __init__(self, name):
        self.name = name
        self.last_w = None
        self.readers = []


class Rec:
    def __init__(self):
        self.ops = []
        self.bufs = {}

    def buf(self, name):
        if name not in self.bufs:
            self.bufs[name] = Buf(name)
        return self.bufs[name]

    def op(self, engine, emit, reads=(), writes=(), dma=False):
        rb = [self.buf(x) for x in reads]
        wb = [self.buf(x) for x in writes]
        deps = set()
        for b in rb:
            if b.last_w is not None:
                deps.add(b.last_w)
        for b in wb:
            if b.last_w is not None:
                deps.add(b.last_w)
            deps.update(b.readers)
        o = Op(engine, emit, deps, dma, len(self.ops))
        self.ops.append(o)
        for b in rb:
            b.readers.append(o)
        for b in wb:
            b.last_w = o
            b.readers = []
        return o

    def fence(self):
        last = {}
        for o in self.ops:
            last[(o.engine, o.is_dma)] = o
        f = Op("dve", lambda e: e.engine_nop(), set(last.values()), False,
               len(self.ops))
        self.ops.append(f)
        for b in self.bufs.values():
            b.last_w = f
            b.readers = []

    SYNC_SAME = ("dve", "act")

    def finalize(self):
        for o in self.ops:
            if o.is_dma:
                o.needed = True
            for d in o.deps:
                if (d.is_dma or d.engine != o.engine
                        or d.engine in self.SYNC_SAME):
                    d.needed = True
        last = {}
        for o in self.ops:
            last[(o.engine, o.is_dma)] = o
        for o in last.values():
            o.needed = True
        cnt = {}
        for o in self.ops:
            if o.needed:
                k = (o.engine, o.is_dma)
                cnt[k] = cnt.get(k, 0) + 1
                o.rank = cnt[k]
        self.final = cnt

    def waits_for(self, o):
        best = {}
        for d in o.deps:
            if not (d.is_dma or d.engine != o.engine
                    or d.engine in self.SYNC_SAME):
                continue
            k = (d.engine, d.is_dma)
            v = d.rank * (16 if d.is_dma else 1)
            if v > best.get(k, -1):
                best[k] = v
        return best


def build_program():
    nc = bass.Bass("TRN2", target_bir_lowering=False)
    E = {}

    def din(name, shape, dtype=F32):
        E[name] = nc.dram_tensor(name, list(shape), dtype, kind="ExternalInput")

    def dint(name, shape, dtype):
        E[name] = nc.dram_tensor(name, list(shape), dtype, kind="Internal")

    din("x", (TOK, DM))
    din("wr", (128, 128))
    din("w_inTz", (DM, DI))
    din("w_inTxb", (DI, DM))
    din("w_xpT", (DI, 4416))
    din("w_outT", (DI, DM))
    din("w_dtT", (64, 64))
    din("wgb", (16384, 2048))
    din("wub", (16384, 2048))
    din("wdb", (16384, 2048))
    din("convw", (128, 128))
    din("convb", (128, 32))
    din("dtb", (64, 1))
    din("alog", (64, 1))
    din("dvec", (64, 1))
    din("negi64", (64, 64))
    din("tri", (128, 128), BF16)
    din("id128f", (128, 128))
    din("id128b", (128, 128), BF16)
    din("id64f", (64, 64))
    din("ones64", (64, 128))
    din("ones64b", (64, 128), BF16)
    din("sel8", (8, 1024), BF16)
    din("epsc", (128, 1))
    out_t = nc.dram_tensor("out", [TOK, DM], F32, kind="ExternalOutput")
    dint("s_wgb", (16384, 2048), BF16)
    dint("s_wub", (16384, 2048), BF16)
    dint("s_wdb", (16384, 2048), BF16)
    dint("s_inTz", (DM, DI), BF16)
    dint("s_inTxb", (DI, DM), BF16)
    dint("s_xpT", (DI, 4416), BF16)
    dint("s_outT", (DI, DM), BF16)
    dint("s_ffn", (DM, TOK), BF16)
    dint("s_flat", (64, 512), F32)

    r = Rec()
    op = r.op
    import contextlib
    with contextlib.ExitStack() as es:
        def sbt(name, shape, dtype):
            return es.enter_context(nc.sbuf_tensor(name, list(shape), dtype))

        AB = sbt("AB", [128, AB_SZ], BF16)
        AFt = sbt("AF", [128, AF_SZ], F32)
        c_wr = sbt("c_wr", [128, 128], F32)
        c_negi = sbt("c_negi", [64, 64], F32)
        c_tri = sbt("c_tri", [128, 128], BF16)
        c_idf = sbt("c_idf", [128, 128], F32)
        c_idb = sbt("c_idb", [128, 128], BF16)
        c_id64 = sbt("c_id64", [64, 64], F32)
        c_ones64 = sbt("c_ones64", [64, 128], F32)
        c_ones64b = sbt("c_ones64b", [64, 128], BF16)
        c_sel8 = sbt("c_sel8", [8, 1024], BF16)
        c_convw = sbt("c_convw", [128, 128], F32)
        c_convb = sbt("c_convb", [128, 32], F32)
        c_dtb = sbt("c_dtb", [64, 1], F32)
        c_alog = sbt("c_alog", [64, 1], F32)
        c_dvec = sbt("c_dvec", [64, 1], F32)
        negA = sbt("negA", [64, 1], F32)
        c_wdtT = sbt("c_wdtT", [64, 64], F32)
        c_eps = sbt("c_eps", [128, 1], F32)
        haloT = sbt("haloT", [128, 96], BF16)
        stats = sbt("stats", [128, 24], F32)

        pA = es.enter_context(nc.psum_tensor("pA", [128, 512], F32))
        pB = es.enter_context(nc.psum_tensor("pB", [128, 512], F32))
        pC = es.enter_context(nc.psum_tensor("pC", [128, 512], F32))
        pD = es.enter_context(nc.psum_tensor("pD", [128, 512], F32))
        pE = es.enter_context(nc.psum_tensor("pE", [128, 512], F32))
        pF = es.enter_context(nc.psum_tensor("pF", [128, 512], F32))
        pG = es.enter_context(nc.psum_tensor("pG", [128, 512], F32))
        pT = es.enter_context(nc.psum_tensor("pT", [128, 1024], BF16))
        PS = {"pA": pA, "pB": pB, "pC": pC, "pD": pD, "pE": pE, "pF": pF,
              "pG": pG, "pT": pT}

        sems = {}
        for n in ("cPE", "cACT", "cDVE", "dSP", "dGP"):
            sems[n] = es.enter_context(nc.semaphore(n))

        def ab(off, cols, p0=0, pn=128):
            return AB[p0:p0 + pn, off:off + cols]

        def af(off, cols, p0=0, pn=128):
            return AFt[p0:p0 + pn, off:off + cols]

        def xbrn(off, ln):
            lo = off // 516
            hi = min(32, (off + ln - 1) // 516 + 1)
            return tuple(f"xbr{c}" for c in range(lo, hi))

        def cast_piece(dstname, srcname, a, b, bufname):
            op("gp", lambda e, a=a, b=b, d=E[dstname], s=E[srcname]:
               e.dma_start(d[a:b, :], s[a:b, :]),
               reads=(), writes=(bufname,), dma=True)

        for e_ in range(8):
            for t_, src, dst in (("g", "wgb", "s_wgb"), ("u", "wub", "s_wub"),
                                 ("d", "wdb", "s_wdb")):
                cast_piece(dst, src, e_ * 2048, (e_ + 1) * 2048, f"sw{t_}{e_}")
        if STAGE == "full":
            for nm, src, rows in (("s_inTxb", "w_inTxb", DI),
                                  ("s_inTz", "w_inTz", DM),
                                  ("s_xpT", "w_xpT", DI),
                                  ("s_outT", "w_outT", DI)):
                for i in range(0, rows, 2048):
                    cast_piece(nm, src, i, min(i + 2048, rows), nm)

        def load(dst_ap, srcname, bufname):
            op("sp", lambda e, d=dst_ap, s=E[srcname]: e.dma_start(d, s[:, :]),
               reads=(), writes=(bufname,), dma=True)

        for t, nm in ((c_wr, "wr"), (c_negi, "negi64"), (c_tri, "tri"),
                      (c_idf, "id128f"), (c_idb, "id128b"), (c_id64, "id64f"),
                      (c_ones64, "ones64"), (c_ones64b, "ones64b"),
                      (c_sel8, "sel8"), (c_convw, "convw"), (c_convb, "convb"),
                      (c_dtb, "dtb"), (c_alog, "alog"), (c_dvec, "dvec"),
                      (c_wdtT, "w_dtT"), (c_eps, "epsc")):
            load(t[:, :], nm, "c_" + nm)

        op("act", lambda e: e.activation(negA[:, :], c_alog[:, :], AF.Exp),
           reads=("c_alog",), writes=("negA",))
        op("dve", lambda e: e.tensor_scalar(negA[:, :], negA[:, :], -1.0, None,
                                            ALU.mult),
           reads=("negA",), writes=("negA",))

        def ln_tile(tok0, router_k, evac, sinkname=("gu",), par=0):
            sc = par * 12
            stn = f"stats{par}"
            xl = (af(XLOAD, 2048) if par == 0
                  else ab(XL2AB, 4096).bitcast(F32))
            xln = f"xl{par}"
            wr2 = (xln,) if par == 0 else (xln,) + tuple(sinkname)
            op("sp", lambda e, xl=xl: e.dma_start(xl,
                                                  E["x"][tok0:tok0 + 128, :]),
               reads=(), writes=wr2, dma=True)
            op("act", lambda e: e.activation(ab(GU, 2048), xl, AF.Square,
                                             accum_out=stats[:, sc + 1:sc + 2]),
               reads=(xln,), writes=tuple(sinkname) + (stn,))
            op("dve", lambda e: e.reduce_sum(stats[:, sc:sc + 1], xl,
                                             axis=AX.X),
               reads=(xln,), writes=(stn,))
            op("dve", lambda e: e.tensor_scalar(
                stats[:, sc + 2:sc + 3], stats[:, sc:sc + 1],
                1.0 / DM, None, ALU.mult),
               reads=(stn,), writes=(stn,))
            op("dve", lambda e: e.tensor_scalar(
                stats[:, sc + 3:sc + 4], stats[:, sc + 1:sc + 2],
                1.0 / DM, None, ALU.mult),
               reads=(stn,), writes=(stn,))
            op("dve", lambda e: e.tensor_tensor(
                stats[:, sc + 4:sc + 5], stats[:, sc + 2:sc + 3],
                stats[:, sc + 2:sc + 3], ALU.mult),
               reads=(stn,), writes=(stn,))
            op("dve", lambda e: e.tensor_tensor(
                stats[:, sc + 5:sc + 6], stats[:, sc + 3:sc + 4],
                stats[:, sc + 4:sc + 5], ALU.subtract),
               reads=(stn,), writes=(stn,))
            op("act", lambda e: e.activation(
                stats[:, sc + 6:sc + 7], stats[:, sc + 5:sc + 6],
                AF.Sqrt, bias=c_eps[:, 0:1]),
               reads=(stn, "c_epsc"), writes=(stn,))
            op("dve", lambda e: e.reciprocal(stats[:, sc + 7:sc + 8],
                                             stats[:, sc + 6:sc + 7]),
               reads=(stn,), writes=(stn,))
            op("dve", lambda e: e.tensor_scalar(
                af(XNTM, 2048), xl, stats[:, sc + 2:sc + 3],
                stats[:, sc + 7:sc + 8], ALU.subtract, ALU.mult),
               reads=(xln, stn), writes=("xntm",))
            for k in range(16):
                p, pn = (pA, "pA") if k % 2 == 0 else (pB, "pB")
                op("pe", lambda e, k=k, p=p: e.transpose(
                    p[:, 0:128], af(XNTM + 128 * k, 128), c_idf[:, :]),
                   reads=("xntm", "c_id128f"), writes=(pn,))
                if router_k:
                    op("dve", lambda e, p=p: e.tensor_copy(af(WBC, 128),
                                                           p[:, 0:128]),
                       reads=(pn,), writes=("wbc",))
                    op("pe", lambda e, k=k: e.matmul(
                        pG[0:8, 0:128], c_wr[:, 8 * k:8 * k + 8], af(WBC, 128),
                        start=(k == 0), stop=(k == 15)),
                       reads=("wbc", "c_wr"), writes=("pG",))
                evac(k, p, pn)

        for half in range(2):
            ht0 = half * 2048
            for j in range(16):
                def mo_evac(k, p, pn, j=j):
                    op("dve", lambda e, k=k, j=j, p=p: e.tensor_copy(
                        ab(XNH + k * 2048 + j * 128, 128), p[:, 0:128]),
                       reads=(pn,), writes=("xnh",))
                ln_tile(ht0 + j * 128, True, mo_evac, par=j % 2)
                op("dve", lambda e: e.tensor_copy(af(W8A, 128, 0, 8), pG[0:8, 0:128]),
                   reads=("pG",), writes=("lgf",))
                op("pe", lambda e: e.transpose(pG[0:128, 0:8],
                                               af(W8A, 128, 0, 8),
                                               c_idf[0:8, 0:8]),
                   reads=("lgf", "c_id128f"), writes=("pG",))
                op("dve", lambda e, j=j: e.tensor_copy(af(RL + 8 * j, 8),
                                                       pG[0:128, 0:8]),
                   reads=("pG",), writes=("rtr",))
                op("dve", lambda e, j=j: e.max(out=af(MX8, 8),
                                               in_=af(RL + 8 * j, 8)),
                   reads=("rtr",), writes=("rtr",))
                op("dve", lambda e, j=j: e.tensor_copy(af(M1C + j, 1),
                                                       af(MX8, 1)),
                   reads=("rtr",), writes=("rtr",))
                op("dve", lambda e, j=j: e.tensor_copy(af(M2C + j, 1),
                                                       af(MX8 + 1, 1)),
                   reads=("rtr",), writes=("rtr",))
                op("dve", lambda e, j=j: e.tensor_tensor(
                    af(DLT + j, 1), af(M2C + j, 1), af(M1C + j, 1),
                    ALU.subtract),
                   reads=("rtr",), writes=("rtr",))
            op("act", lambda e: e.activation(af(EXC, 16), af(DLT, 16), AF.Exp),
               reads=("rtr",), writes=("rtr",))
            op("dve", lambda e: e.tensor_scalar(af(EXC, 16), af(EXC, 16), 1.0,
                                                None, ALU.add),
               reads=("rtr",), writes=("rtr",))
            op("dve", lambda e: e.reciprocal(af(P1C, 16), af(EXC, 16)),
               reads=("rtr",), writes=("rtr",))
            op("dve", lambda e: e.tensor_scalar(af(P2C, 16), af(P1C, 16), -1.0,
                                                1.0, ALU.mult, ALU.add),
               reads=("rtr",), writes=("rtr",))
            for j in range(16):
                op("dve", lambda e, j=j: e.tensor_single_scalar(
                    af(EQ1, 8), af(RL + 8 * j, 8), af(M1C + j, 1),
                    ALU.is_equal),
                   reads=("rtr",), writes=("rtr8",))
                op("dve", lambda e, j=j: e.tensor_single_scalar(
                    af(EQ2, 8), af(RL + 8 * j, 8), af(M2C + j, 1),
                    ALU.is_equal),
                   reads=("rtr", "rtr8"), writes=("rtr8",))
                op("dve", lambda e, j=j: e.tensor_scalar(
                    af(V1C, 8), af(EQ1, 8), af(P1C + j, 1), None, ALU.mult),
                   reads=("rtr8", "rtr"), writes=("rtr8",))
                op("dve", lambda e, j=j: e.scalar_tensor_tensor(
                    af(W8A + 8 * j, 8), af(EQ2, 8), af(P2C + j, 1),
                    af(V1C, 8), ALU.mult, ALU.add),
                   reads=("rtr8", "rtr"), writes=("w8",))
                op("pe", lambda e, j=j: e.transpose(
                    pG[0:8, 0:128], af(W8A + 8 * j, 8), c_idf[:, :]),
                   reads=("w8", "c_id128f"), writes=("pG",))
                op("dve", lambda e, j=j: e.tensor_copy(
                    ab(WFM + j * 128, 128, 0, 8), pG[0:8, 0:128]),
                   reads=("pG",), writes=("wfm",))

            for tbD in range(2):
                t0 = tbD * 1024
                for e_ in range(8):
                    for t5 in range(2):
                        tt0 = t0 + t5 * 512
                        wb_off = WBC if t5 == 0 else CUMA
                        wbn = f"wbc{t5}"
                        op("pe", lambda e, e_=e_, tt0=tt0: e.matmul(
                            pG[:, 0:512],
                            c_sel8[:, e_ * 128:(e_ + 1) * 128],
                            ab(WFM + tt0, 512, 0, 8),
                            start=True, stop=True),
                           reads=("wfm", "c_sel8"), writes=("pG",))
                        op("dve", lambda e, wb_off=wb_off: e.tensor_copy(
                            af(wb_off, 512), pG[:, 0:512]),
                           reads=("pG",), writes=(wbn,))
                    for ic in range(16):
                        ws = WS0 if ic % 2 == 0 else WS1
                        wsn = f"wg{ic % 2}"
                        op("sp", lambda e, e_=e_, ic=ic, ws=ws: e.dma_start(
                            ab(ws, 2048),
                            E["s_wgb"][e_ * 2048 + ic * 128:
                                       e_ * 2048 + ic * 128 + 128, :]),
                           reads=(f"swg{e_}",), writes=(wsn,), dma=True)
                        op("sp", lambda e, e_=e_, ic=ic: e.dma_start(
                            ab(WU0, 2048),
                            E["s_wub"][e_ * 2048 + ic * 128:
                                       e_ * 2048 + ic * 128 + 128, :]),
                           reads=(f"swu{e_}",), writes=("wu0",), dma=True)
                        for k in range(16):
                            for t5 in range(2):
                                pg, pgn = (pA, "pA") if t5 == 0 else (pB, "pB")
                                op("pe", lambda e, k=k, ws=ws, t5=t5, pg=pg,
                                   t0=t0: e.matmul(
                                       pg[:, 0:512],
                                       ab(ws + k * 128, 128),
                                       ab(XNH + k * 2048 + t0 + t5 * 512, 512),
                                       start=(k == 0), stop=(k == 15)),
                                   reads=(wsn, "xnh"), writes=(pgn,))
                        for t5 in range(2):
                            pg, pgn = (pA, "pA") if t5 == 0 else (pB, "pB")
                            tg = TMPG if t5 == 0 else TMPU
                            tgn = f"tmpg{t5}"
                            op("act", lambda e, pg=pg, tg=tg: e.activation(
                                ab(tg, 512), pg[:, 0:512], AF.Silu),
                               reads=(pgn,), writes=(tgn,))
                            op("dve", lambda e, tg=tg: e.tensor_scalar(
                                ab(tg, 512), ab(tg, 512), 7.0, -7.0,
                                ALU.min, ALU.max),
                               reads=(tgn,), writes=(tgn,))
                        for k in range(16):
                            for t5 in range(2):
                                pu = pC if t5 == 0 else pD
                                pun = "pC" if t5 == 0 else "pD"
                                op("pe", lambda e, k=k, t5=t5, pu=pu, t0=t0:
                                   e.matmul(
                                       pu[:, 0:512],
                                       ab(WU0 + k * 128, 128),
                                       ab(XNH + k * 2048 + t0 + t5 * 512, 512),
                                       start=(k == 0), stop=(k == 15)),
                                   reads=("wu0", "xnh"), writes=(pun,))
                        for t5 in range(2):
                            pu = pC if t5 == 0 else pD
                            pun = "pC" if t5 == 0 else "pD"
                            tg = TMPG if t5 == 0 else TMPU
                            tgn = f"tmpg{t5}"
                            wb_off = WBC if t5 == 0 else CUMA
                            wbn = f"wbc{t5}"
                            op("dve", lambda e, pu=pu, wb_off=wb_off:
                               e.tensor_tensor(
                                   ab(TMPGC, 512), pu[:, 0:512],
                                   af(wb_off, 512), ALU.mult),
                               reads=(pun, wbn), writes=("tmpgc",))
                            op("dve", lambda e, ic=ic, t5=t5, tg=tg:
                               e.tensor_tensor(
                                   ab(GU + ic * 1024 + t5 * 512, 512),
                                   ab(tg, 512), ab(TMPGC, 512), ALU.mult),
                               reads=(tgn, "tmpgc"), writes=("gu",))
                    for oc in range(16):
                        ws = WS0 if oc % 2 == 0 else WS1
                        wsn = f"wg{oc % 2}"
                        op("sp", lambda e, e_=e_, oc=oc, ws=ws: e.dma_start(
                            ab(ws, 2048),
                            E["s_wdb"][e_ * 2048 + oc * 128:
                                       e_ * 2048 + oc * 128 + 128, :]),
                           reads=(f"swd{e_}",), writes=(wsn,), dma=True)
                        for ic in range(16):
                            for t5 in range(2):
                                pd_ = pE if t5 == 0 else pF
                                pdn = "pE" if t5 == 0 else "pF"
                                op("pe", lambda e, ic=ic, t5=t5, ws=ws,
                                   pd_=pd_: e.matmul(
                                       pd_[:, 0:512], ab(ws + ic * 128, 128),
                                       ab(GU + ic * 1024 + t5 * 512, 512),
                                       start=(ic == 0), stop=(ic == 15)),
                                   reads=(wsn, "gu"), writes=(pdn,))
                        for t5 in range(2):
                            pd_ = pE if t5 == 0 else pF
                            pdn = "pE" if t5 == 0 else "pF"
                            if e_ == 0:
                                op("dve", lambda e, oc=oc, t5=t5, pd_=pd_:
                                   e.tensor_copy(
                                       ab(ACC + oc * 1024 + t5 * 512, 512),
                                       pd_[:, 0:512]),
                                   reads=(pdn,), writes=("acc",))
                            else:
                                op("dve", lambda e, oc=oc, t5=t5, pd_=pd_:
                                   e.tensor_tensor(
                                       ab(ACC + oc * 1024 + t5 * 512, 512),
                                       ab(ACC + oc * 1024 + t5 * 512, 512),
                                       pd_[:, 0:512], ALU.add),
                                   reads=("acc", pdn), writes=("acc",))
                c0 = ht0 + t0
                dst = bass.AP(E["s_ffn"], c0,
                              [[4096, 128], [128 * 4096, 16], [1, 1024]])
                op("gp", lambda e, dst=dst: e.dma_start(dst, ab(ACC, 16384)),
                   reads=("acc",), writes=("sffn",), dma=True)

        if DBG:
            allb0 = tuple(r.bufs.keys())
            for i, (ar, off, cols, p0, pn) in enumerate(DBG):
                dbg_t = nc.dram_tensor(f"dbg{i}", [pn, cols], F32,
                                       kind="ExternalOutput")
                src = (ab(off, cols, p0, pn) if ar == "ab"
                       else af(off, cols, p0, pn))
                op("gp", lambda e, d=dbg_t, s_=src: e.dma_start(d[:, :], s_),
                   reads=allb0, writes=(f"dbg{i}",), dma=True)

        if STAGE == "full":
            r.fence()

            op("dve", lambda e: e.tensor_scalar(
                ab(EALL, 4096, 0, 64),
                bass.AP(c_id64, 0, [[64, 64], [1, 64], [0, 64]]),
                c_dvec[:, 0:1], None, ALU.mult),
               reads=("c_id64f", "c_dvec"), writes=("eall",))
            for q in range(8):
                op("pe", lambda e, q=q: e.matmul(
                    pA[:, 0:512], c_ones64b[:, :], ab(EALL + 512 * q, 512,
                                                      0, 64),
                    start=True, stop=True),
                   reads=("eall", "c_ones64b"), writes=("pA",))
                op("dve", lambda e, q=q: e.tensor_copy(ab(DBC + 512 * q, 512),
                                                       pA[:, 0:512]),
                   reads=("pA",), writes=("dbc",))

            for o_ in range(N_OUT):
                ot0 = o_ * OUT_T
                seq_start = (o_ % 4 == 0)
                for j in range(4):
                    def ma_evac(k, p, pn, j=j):
                        op("dve", lambda e, k=k, j=j, p=p: e.tensor_copy(
                            ab(XNT + k * 512 + j * 128, 128), p[:, 0:128]),
                           reads=(pn,), writes=("xnt",))
                    ln_tile(ot0 + j * 128, False, ma_evac,
                            sinkname=xbrn(8192, 2048) + xbrn(12288, 4096),
                            par=j % 2)

                if seq_start:
                    op("dve", lambda e: e.memset(haloT[:, :], 0.0),
                       reads=(), writes=("halo",))
                for ch in range(32):
                    op("dve", lambda e, ch=ch: e.tensor_copy(
                        ab(XBR + 516 * ch + 1, 3), haloT[:, 3 * ch:3 * ch + 3]),
                       reads=("halo",), writes=(f"xbr{ch}",))

                for oc in range(32):
                    ws = WIN0 if oc % 2 == 0 else WIN1
                    wsn = f"win{oc % 2}"
                    op("sp", lambda e, oc=oc, ws=ws: e.dma_start(
                        ab(ws, 2048),
                        E["s_inTxb"][oc * 128:(oc + 1) * 128, :]),
                       reads=("s_inTxb",), writes=(wsn,), dma=True)
                    p, pn = (pA, "pA") if oc % 2 == 0 else (pB, "pB")
                    for k in range(16):
                        op("pe", lambda e, k=k, ws=ws, p=p: e.matmul(
                            p[:, 0:512], ab(ws + k * 128, 128),
                            ab(XNT + k * 512, 512),
                            start=(k == 0), stop=(k == 15)),
                           reads=(wsn, "xnt"), writes=(pn,))
                    op("dve", lambda e, ch=oc, p=p: e.tensor_copy(
                        ab(XBR + 516 * ch + 4, 512), p[:, 0:512]),
                       reads=(pn,), writes=(f"xbr{oc}",))

                for ch in range(32):
                    op("dve", lambda e, ch=ch: e.tensor_copy(
                        haloT[:, 3 * ch:3 * ch + 3],
                        ab(XBR + 516 * ch + 513, 3)),
                       reads=(f"xbr{ch}",), writes=("halo",))
                    op("dve", lambda e, ch=ch: e.tensor_scalar(
                        af(CT0, 512), ab(XBR + 516 * ch + 1, 512),
                        c_convw[:, 4 * ch:4 * ch + 1], None, ALU.mult),
                       reads=(f"xbr{ch}", "c_convw"), writes=("xntm",))
                    for k in range(1, 4):
                        src, dst = (CT0, CT1) if k % 2 == 1 else (CT1, CT0)
                        op("dve", lambda e, ch=ch, k=k, src=src, dst=dst:
                           e.scalar_tensor_tensor(
                               af(dst, 512), ab(XBR + 516 * ch + 1 + k, 512),
                               c_convw[:, 4 * ch + k:4 * ch + k + 1],
                               af(src, 512), ALU.mult, ALU.add),
                           reads=(f"xbr{ch}", "c_convw", "xntm"),
                           writes=("xntm",))
                    op("act", lambda e, ch=ch: e.activation(
                        ab(XBR + 516 * ch + 4, 512), af(CT1, 512), AF.Silu,
                        bias=c_convb[:, ch:ch + 1]),
                       reads=("xntm", "c_convb"), writes=(f"xbr{ch}",))

                for osl in range(8):
                    for kg in range(4):
                        ws = WXP0 if kg % 2 == 0 else WXP1
                        wsn = f"wxp{kg % 2}"
                        src = bass.AP(
                            E["s_xpT"], kg * 1024 * 4416 + osl * 512,
                            [[4416, 128], [128 * 4416, 8], [1, 512]])
                        op("sp", lambda e, ws=ws, src=src: e.dma_start(
                            ab(ws, 4096), src),
                           reads=("s_xpT",), writes=(wsn,), dma=True)
                        for k8 in range(8):
                            k = kg * 8 + k8
                            for tl in range(4):
                                pp = (pC, pD, pE, pF)[tl]
                                ppn = ("pC", "pD", "pE", "pF")[tl]
                                op("pe", lambda e, k=k, k8=k8, tl=tl, ws=ws,
                                   pp=pp: e.matmul(
                                       pp[:, 0:512],
                                       ab(XBR + 516 * k + 4 + tl * 128, 128),
                                       ab(ws + k8 * 512, 512),
                                       start=(k == 0), stop=(k == 31)),
                                   reads=(wsn, f"xbr{k}"), writes=(ppn,))
                    for tl in range(4):
                        pp = (pC, pD, pE, pF)[tl]
                        ppn = ("pC", "pD", "pE", "pF")[tl]
                        op("dve", lambda e, tl=tl, osl=osl, pp=pp:
                           e.tensor_copy(
                               ab(XTM + tl * 4096 + osl * 512, 512),
                               pp[:, 0:512]),
                           reads=(ppn,), writes=("xtm",))

                kgroups = [list(range(g, min(g + 3, 32)))
                           for g in range(0, 32, 3)]
                for gi, grp in enumerate(kgroups):
                    ws = WTL0 if gi % 2 == 0 else WTL1
                    wsn = f"wtl{gi % 2}"
                    k0, nk = grp[0], len(grp)
                    src = bass.AP(E["s_xpT"], k0 * 128 * 4416 + 4096,
                                  [[4416, 128], [128 * 4416, nk], [1, 320]])
                    op("sp", lambda e, ws=ws, src=src, nk=nk: e.dma_start(
                        ab(ws, nk * 320), src),
                       reads=("s_xpT",),
                       writes=(wsn, "bfm", "cfm", "m0", "bt"), dma=True)
                    for ji, k in enumerate(grp):
                        wo = ws + ji * 320
                        op("pe", lambda e, k=k, wo=wo: e.matmul(
                            pA[0:64, 0:512], ab(wo, 64),
                            ab(XBR + 516 * k + 4, 512),
                            start=(k == 0), stop=(k == 31)),
                           reads=(wsn, f"xbr{k}"), writes=("pA",))
                        op("pe", lambda e, k=k, wo=wo: e.matmul(
                            pB[:, 0:512], ab(wo + 64, 128),
                            ab(XBR + 516 * k + 4, 512),
                            start=(k == 0), stop=(k == 31)),
                           reads=(wsn, f"xbr{k}"), writes=("pB",))
                        op("pe", lambda e, k=k, wo=wo: e.matmul(
                            pG[:, 0:512], ab(wo + 192, 128),
                            ab(XBR + 516 * k + 4, 512),
                            start=(k == 0), stop=(k == 31)),
                           reads=(wsn, f"xbr{k}"), writes=("pG",))
                op("dve", lambda e: e.tensor_copy(af(DTRAW, 512, 0, 64),
                                                  pA[0:64, 0:512]),
                   reads=("pA",), writes=("dtraw",))
                op("dve", lambda e: e.tensor_copy(ab(BFMO, 512), pB[:, 0:512]),
                   reads=("pB",), writes=("bfm",))
                op("dve", lambda e: e.tensor_copy(ab(CFMO, 512), pG[:, 0:512]),
                   reads=("pG",), writes=("cfm",))

                op("pe", lambda e: e.matmul(pC[0:64, 0:512], c_wdtT[:, :],
                                            af(DTRAW, 512, 0, 64),
                                            start=True, stop=True),
                   reads=("dtraw", "c_w_dtT"), writes=("pC",))
                op("act", lambda e: e.activation(af(TMPT, 512, 0, 64),
                                                 pC[0:64, 0:512], AF.Abs,
                                                 bias=c_dtb[:, 0:1]),
                   reads=("pC", "c_dtb"), writes=("tmpt",))
                op("act", lambda e: e.activation(af(YDS, 512, 0, 64),
                                                 af(TMPT, 512, 0, 64), AF.Exp,
                                                 scale=-1.0),
                   reads=("tmpt",), writes=("yds",))
                op("act", lambda e: e.activation(af(TMPT, 512, 0, 64),
                                                 af(YDS, 512, 0, 64), AF.Ln,
                                                 bias=1.0),
                   reads=("yds",), writes=("tmpt",))
                op("act", lambda e: e.activation(af(YDS, 512, 0, 64),
                                                 pC[0:64, 0:512], AF.Relu,
                                                 bias=c_dtb[:, 0:1]),
                   reads=("pC", "c_dtb"), writes=("yds",))
                op("dve", lambda e: e.tensor_tensor(af(DT2, 512, 0, 64),
                                                    af(TMPT, 512, 0, 64),
                                                    af(YDS, 512, 0, 64),
                                                    ALU.add),
                   reads=("tmpt", "yds"), writes=("dt2",))
                op("dve", lambda e: e.tensor_scalar(af(DTRAW, 512, 0, 64),
                                                    af(DT2, 512, 0, 64),
                                                    negA[:, 0:1], None,
                                                    ALU.mult),
                   reads=("dt2", "negA"), writes=("dtraw",))
                for c in range(4):
                    op("dve", lambda e, c=c: e.tensor_tensor_scan(
                        af(CUMA + 128 * c, 128, 0, 64),
                        af(DTRAW + 128 * c, 128, 0, 64),
                        af(DTRAW + 128 * c, 128, 0, 64), 0.0,
                        ALU.add, ALU.bypass),
                       reads=("dtraw",), writes=("cuma",))
                op("gp", lambda e: e.dma_start(E["s_flat"][:, :],
                                               af(CUMA, 512, 0, 64)),
                   reads=("cuma",), writes=("sflat",), dma=True)

                if seq_start:
                    op("dve", lambda e: e.memset(af(S0, 4096), 0.0),
                       reads=(), writes=("S",))
                    op("dve", lambda e: e.memset(ab(SBF, 4096), 0.0),
                       reads=(), writes=("sbf",))

                op("act", lambda e: e.activation(ab(SBF, 2048),
                                                 af(S0, 2048), AF.Identity),
                   reads=("S",), writes=("sbf",))
                op("act", lambda e: e.activation(ab(SBF + 2048, 2048),
                                                 af(S0 + 2048, 2048),
                                                 AF.Identity),
                   reads=("S",), writes=("sbf",))

                for c in range(4):
                    for js in range(16):
                        fb = FLT0 + (js % 2) * 512
                        fbn = f"flt{js % 2}"
                        src = bass.AP(E["s_flat"], 4 * js * 512 + 128 * c,
                                      [[0, 128], [512, 4], [1, 128]])
                        op("gp", lambda e, src=src, fb=fb: e.dma_start(
                            af(fb, 512), src),
                           reads=("sflat",), writes=(fbn,), dma=True)
                        p, pn = (pA, "pA") if js % 2 == 0 else (pB, "pB")
                        rhs_negi = bass.AP(c_negi, 4 * js,
                                           [[64, 64], [1, 4], [0, 128]])
                        op("pe", lambda e, c=c, p=p, rn=rhs_negi: e.matmul(
                            p[:, 0:512], af(CUMA + 128 * c, 128, 0, 64), rn,
                            start=True, stop=True),
                           reads=("cuma", "c_negi64"), writes=(pn,))
                        op("dve", lambda e, p=p: e.tensor_tensor(
                            af(TMPT, 512), p[:, 0:512],
                            bass.AP(c_tri, 0, [[128, 128], [0, 4], [1, 128]]),
                            ALU.add),
                           reads=(pn, "c_tri"), writes=("tmpt",))
                        op("dve", lambda e, fb=fb: e.tensor_tensor(
                            af(TMPT, 512), af(TMPT, 512), af(fb, 512),
                            ALU.add),
                           reads=("tmpt", fbn), writes=("tmpt",))
                        op("act", lambda e, js=js: e.activation(
                            ab(EALL + 512 * js, 512), af(TMPT, 512), AF.Exp),
                           reads=("tmpt",), writes=("eall",))
                    op("pe", lambda e, c=c: e.transpose(
                        pC[:, 0:64], af(CUMA + 128 * c, 128, 0, 64),
                        c_id64[:, :]),
                       reads=("cuma", "c_id64f"), writes=("pC",))
                    op("dve", lambda e: e.tensor_copy(af(CUMAT, 64),
                                                      pC[:, 0:64]),
                       reads=("pC",), writes=("cumat",))
                    op("act", lambda e: e.activation(af(SDOT, 64),
                                                     af(CUMAT, 64), AF.Exp),
                       reads=("cumat",), writes=("sdot",))
                    op("pe", lambda e, c=c: e.transpose(
                        pC[:, 64:128], af(DT2 + 128 * c, 128, 0, 64),
                        c_id64[:, :]),
                       reads=("dt2", "c_id64f"), writes=("pC",))
                    op("dve", lambda e: e.tensor_copy(af(DTT, 64),
                                                      pC[:, 64:128]),
                       reads=("pC",), writes=("dtt",))
                    op("dve", lambda e, c=c: e.tensor_scalar(
                        af(TMPDS, 128, 0, 64), af(CUMA + 128 * c, 128, 0, 64),
                        -1.0, af(CUMA + 128 * c + 127, 1, 0, 64),
                        ALU.mult, ALU.add),
                       reads=("cuma",), writes=("tmpds",))
                    op("act", lambda e: e.activation(af(DS2, 128, 0, 64),
                                                     af(TMPDS, 128, 0, 64),
                                                     AF.Exp),
                       reads=("tmpds",), writes=("ds2",))
                    op("pe", lambda e: e.transpose(pD[:, 0:64],
                                                   af(DS2, 128, 0, 64),
                                                   c_id64[:, :]),
                       reads=("ds2", "c_id64f"), writes=("pD",))
                    op("dve", lambda e: e.tensor_copy(af(DSTT, 64),
                                                      pD[:, 0:64]),
                       reads=("pD",), writes=("dstt",))
                    op("act", lambda e, c=c: e.activation(
                        af(CDE2, 1, 0, 64), af(CUMA + 128 * c + 127, 1, 0, 64),
                        AF.Exp),
                       reads=("cuma",), writes=("cde2",))
                    op("dve", lambda e: e.tensor_scalar(
                        af(CDED, 64, 0, 64), c_id64[:, :], af(CDE2, 1, 0, 64),
                        None, ALU.mult),
                       reads=("cde2", "c_id64f"), writes=("cded",))
                    op("pe", lambda e: e.matmul(pD[:, 64:128], c_ones64[:, :],
                                                af(CDED, 64, 0, 64),
                                                start=True, stop=True),
                       reads=("cded", "c_ones64"), writes=("pD",))
                    op("dve", lambda e: e.tensor_copy(af(CDEB, 64),
                                                      pD[:, 64:128]),
                       reads=("pD",), writes=("cdeb",))
                    op("pe", lambda e, c=c: e.matmul(
                        pD[:, 128:256], ab(BFMO + 128 * c, 128),
                        ab(CFMO + 128 * c, 128), start=True, stop=True),
                       reads=("bfm", "cfm"), writes=("pD",))
                    op("dve", lambda e: e.tensor_copy(ab(M0O, 128), pD[:, 128:256]),
                       reads=("pD",), writes=("m0",))
                    op("pe", lambda e, c=c: e.transpose(
                        pT[:, 0:128], ab(BFMO + 128 * c, 128),
                        c_idb[:, :]),
                       reads=("bfm", "c_id128b"), writes=("pT",))
                    op("dve", lambda e: e.tensor_copy(ab(BTO, 128), pT[:, 0:128]),
                       reads=("pT",), writes=("bt",))

                    for g8 in range(8):
                        pyd, pydn = (pE, "pE") if g8 % 2 == 0 else (pF, "pF")
                        hs = [8 * g8 + i for i in range(8)]
                        for h in hs:
                            op("dve", lambda e, h=h: e.tensor_tensor(
                                ab(EALL + 128 * h, 128), ab(EALL + 128 * h, 128),
                                ab(M0O, 128), ALU.mult),
                               reads=("eall", "m0"), writes=("eall",))
                            op("dve", lambda e, h=h, c=c: e.tensor_scalar(
                                ab(XS + 64 * h, 64),
                                ab(XTM + c * 4096 + 64 * h, 64),
                                af(DTT + h, 1), None, ALU.mult),
                               reads=("xtm", "dtt"), writes=("xs",))
                        for h in hs:
                            op("pe", lambda e, h=h, pyd=pyd: e.matmul(
                                pyd[:, 64 * (h % 8):64 * (h % 8) + 64],
                                ab(EALL + 128 * h, 128), ab(XS + 64 * h, 64),
                                start=True, stop=True),
                               reads=("eall", "xs"), writes=(pydn,))
                        for h in hs:
                            op("act", lambda e, h=h, pyd=pyd: e.activation(
                                af(YDS + 64 * (h % 8), 64),
                                pyd[:, 64 * (h % 8):64 * (h % 8) + 64],
                                AF.Identity),
                               reads=(pydn,), writes=("yds",))
                        for h in hs:
                            op("dve", lambda e, h=h: e.tensor_scalar(
                                ab(XS + 64 * h, 64), ab(XS + 64 * h, 64),
                                af(DSTT + h, 1), None, ALU.mult),
                               reads=("xs", "dstt"), writes=("xs",))
                        for g4 in range(2):
                            hq = hs[4 * g4:4 * g4 + 4]
                            op("pe", lambda e, c=c, g4=g4, hq=hq: e.matmul(
                                pG[:, 256 * g4:256 * g4 + 256],
                                ab(CFMO + 128 * c, 128),
                                ab(SBF + 64 * hq[0], 256),
                                start=True, stop=True),
                               reads=("cfm", "sbf"), writes=("pG",))
                            for h in hq:
                                op("dve", lambda e, h=h, c=c, g4=g4:
                                   e.scalar_tensor_tensor(
                                       ab(XBR + c * 4096 + 64 * h, 64),
                                       pG[:, 256 * g4 + 64 * (h % 4):
                                          256 * g4 + 64 * (h % 4) + 64],
                                       af(SDOT + h, 1),
                                       af(YDS + 64 * (h % 8), 64),
                                       ALU.mult, ALU.add),
                                   reads=("pG", "sdot", "yds"),
                                   writes=xbrn(c * 4096 + 64 * h, 64))
                            pm = pC if g4 == 0 else pD
                            pmn = "pC" if g4 == 0 else "pD"
                            op("pe", lambda e, hq=hq, pm=pm: e.matmul(
                                pm[:, 256:512], ab(BTO, 128),
                                ab(XS + 64 * hq[0], 256),
                                start=True, stop=True),
                               reads=("bt", "xs"), writes=(pmn,))
                            for h in hq:
                                op("dve", lambda e, h=h, pm=pm:
                                   e.scalar_tensor_tensor(
                                       af(S0 + 64 * h, 64), af(S0 + 64 * h, 64),
                                       af(CDEB + h, 1),
                                       pm[:, 256 + 64 * (h % 4):
                                          256 + 64 * (h % 4) + 64],
                                       ALU.mult, ALU.add),
                                   reads=("S", "cdeb", pmn), writes=("S",))
                    op("act", lambda e: e.activation(ab(SBF, 2048),
                                                     af(S0, 2048), AF.Identity),
                       reads=("S",), writes=("sbf",))
                    op("act", lambda e: e.activation(ab(SBF + 2048, 2048),
                                                     af(S0 + 2048, 2048),
                                                     AF.Identity),
                       reads=("S",), writes=("sbf",))

                for j in range(4):
                    tok0 = ot0 + j * 128
                    for osl in range(8):
                        pz, pzn = (pA, "pA") if osl % 2 == 0 else (pB, "pB")
                        for kg in range(2):
                            ws = WXP0 if kg % 2 == 0 else WXP1
                            wsn = f"wxp{kg % 2}"
                            src = bass.AP(
                                E["s_inTz"], kg * 1024 * 4096 + osl * 512,
                                [[4096, 128], [128 * 4096, 8], [1, 512]])
                            op("sp", lambda e, ws=ws, src=src: e.dma_start(
                                ab(ws, 4096), src),
                               reads=("s_inTz",), writes=(wsn,), dma=True)
                            for k8 in range(8):
                                k = kg * 8 + k8
                                op("pe", lambda e, k=k, k8=k8, j=j, ws=ws,
                                   pz=pz: e.matmul(
                                       pz[:, 0:512],
                                       ab(XNT + k * 512 + j * 128, 128),
                                       ab(ws + k8 * 512, 512),
                                       start=(k == 0), stop=(k == 15)),
                                   reads=(wsn, "xnt"), writes=(pzn,))
                        op("act", lambda e, osl=osl, pz=pz: e.activation(
                            ab(XS + osl * 512, 512), pz[:, 0:512], AF.Silu),
                           reads=(pzn,), writes=("xs",))
                    op("dve", lambda e, j=j: e.tensor_tensor(
                        ab(EALL, 4096), ab(XTM + j * 4096, 4096),
                        ab(DBC, 4096), ALU.mult),
                       reads=("xtm", "dbc"), writes=("eall",))
                    op("dve", lambda e, j=j: e.tensor_tensor(
                        ab(EALL, 4096), ab(EALL, 4096),
                        ab(XBR + j * 4096, 4096), ALU.add),
                       reads=("eall",) + xbrn(j * 4096, 4096),
                       writes=("eall",))
                    op("dve", lambda e: e.tensor_tensor(
                        ab(EALL, 4096), ab(EALL, 4096), ab(XS, 4096),
                        ALU.mult),
                       reads=("eall", "xs"), writes=("eall",))
                    for k in range(32):
                        sl = k % 8
                        op("pe", lambda e, k=k, sl=sl: e.transpose(
                            pT[:, 128 * sl:128 * sl + 128],
                            ab(EALL + 128 * k, 128), c_idb[:, :]),
                           reads=("eall", "c_id128b"), writes=("pT",))
                        op("dve", lambda e, k=k, sl=sl: e.tensor_copy(
                            ab(EALL + 4096 + 128 * k, 128),
                            pT[:, 128 * sl:128 * sl + 128]),
                           reads=("pT",), writes=("eall",))
                    op("sp", lambda e, tok0=tok0: e.dma_start(
                        ab(WIN0, 2048), E["s_ffn"][:, tok0:tok0 + 128],
                        transpose=True),
                       reads=("sffn",), writes=("win0",), dma=True)
                    op("sp", lambda e, tok0=tok0: e.dma_start(
                        af(XLOAD, 2048), E["x"][tok0:tok0 + 128, :]),
                       reads=(), writes=("xl0",), dma=True)
                    for osl in range(4):
                        po, pon = (pC, "pC") if osl % 2 == 0 else (pD, "pD")
                        for kg in range(4):
                            ws = WXP0 if kg % 2 == 0 else WXP1
                            wsn = f"wxp{kg % 2}"
                            src = bass.AP(
                                E["s_outT"], kg * 1024 * 2048 + osl * 512,
                                [[2048, 128], [128 * 2048, 8], [1, 512]])
                            op("sp", lambda e, ws=ws, src=src: e.dma_start(
                                ab(ws, 4096), src),
                               reads=("s_outT",), writes=(wsn,), dma=True)
                            for k8 in range(8):
                                k = kg * 8 + k8
                                op("pe", lambda e, k=k, k8=k8, ws=ws, po=po:
                                   e.matmul(
                                       po[:, 0:512],
                                       ab(EALL + 4096 + 128 * k, 128),
                                       ab(ws + k8 * 512, 512),
                                       start=(k == 0), stop=(k == 31)),
                                   reads=(wsn, "eall"), writes=(pon,))
                        op("dve", lambda e, osl=osl, po=po:
                           e.scalar_tensor_tensor(
                               af(XNTM + osl * 512, 512), po[:, 0:512], 1.0,
                               ab(WIN0 + osl * 512, 512),
                               ALU.mult, ALU.add),
                           reads=(pon, "win0"), writes=("xntm",))
                        op("dve", lambda e, osl=osl: e.tensor_tensor(
                            af(XNTM + osl * 512, 512),
                            af(XNTM + osl * 512, 512),
                            af(XLOAD + osl * 512, 512), ALU.add),
                           reads=("xl0", "xntm"), writes=("xntm",))
                    op("sp", lambda e, tok0=tok0: e.dma_start(
                        out_t[tok0:tok0 + 128, :], af(XNTM, 2048)),
                       reads=("xntm",), writes=("outd",), dma=True)
        else:
            r.fence()
            for t in range(32):
                tok0 = t * 128
                op("sp", lambda e, tok0=tok0: e.dma_start(
                    ab(WIN0, 2048), E["s_ffn"][:, tok0:tok0 + 128], transpose=True),
                   reads=("sffn",), writes=("win0",), dma=True)
                op("sp", lambda e, tok0=tok0: e.dma_start(
                    af(XLOAD, 2048), E["x"][tok0:tok0 + 128, :]),
                   reads=(), writes=("xl",), dma=True)
                op("dve", lambda e: e.tensor_tensor(
                    af(XNTM, 2048), af(XLOAD, 2048), ab(WIN0, 2048), ALU.add),
                   reads=("xl0", "win0"), writes=("xntm",))
                op("sp", lambda e, tok0=tok0: e.dma_start(
                    out_t[tok0:tok0 + 128, :], af(XNTM, 2048)),
                   reads=("xntm",), writes=("outd",), dma=True)

        if DBG2:
            r.fence()
            allb = tuple(r.bufs.keys())
            for i, (ar, off, cols, p0, pn) in enumerate(DBG2):
                dbg_t = nc.dram_tensor(f"dbh{i}", [pn, cols], F32,
                                       kind="ExternalOutput")
                src = (ab(off, cols, p0, pn) if ar == "ab"
                       else af(off, cols, p0, pn))
                op("gp", lambda e, d=dbg_t, s_=src: e.dma_start(d[:, :], s_),
                   reads=allb, writes=(f"dbh{i}",), dma=True)

        r.finalize()
        seq = {"pe": [], "act": [], "dve": [], "sp": [], "gp": []}
        for o in r.ops:
            seq[o.engine].append(o)
        semmap = {("pe", False): sems["cPE"], ("act", False): sems["cACT"],
                  ("dve", False): sems["cDVE"], ("sp", True): sems["dSP"],
                  ("gp", True): sems["dGP"]}

        def run_engine(e, name):
            waited = {}
            for o in seq[name]:
                for k, v in sorted(r.waits_for(o).items()):
                    if waited.get(k, -1) >= v:
                        continue
                    waited[k] = v
                    e.wait_ge(semmap[k], v)
                ins = o.emit(e)
                if o.needed:
                    ins.then_inc(semmap[(o.engine, o.is_dma)],
                                 16 if o.is_dma else 1)
            if name == "sp":
                e.wait_ge(sems["dSP"], 16 * r.final.get(("sp", True), 0))
            if name == "gp":
                e.wait_ge(sems["dGP"], 16 * r.final.get(("gp", True), 0))

        with nc.Block() as block:
            @block.tensor
            def _(e):
                run_engine(e, "pe")

            @block.scalar
            def _(e):
                run_engine(e, "act")

            @block.vector
            def _(e):
                run_engine(e, "dve")

            @block.sync
            def _(e):
                run_engine(e, "sp")

            @block.gpsimd
            def _(e):
                run_engine(e, "gp")

    return nc


_PROG = None


def _get_prog():
    global _PROG
    if _PROG is None:
        _PROG = build_program()
    return _PROG


def _common_inputs(inputs):
    f32 = np.float32
    ipw = np.asarray(inputs["L0_attn_in_proj_w"], f32)
    xpw = np.asarray(inputs["L0_attn_x_proj_w"], f32)
    opw = np.asarray(inputs["L0_attn_out_proj_w"], f32)
    Wr = np.asarray(inputs["L0_ffn_Wr"], f32)
    Wg = np.asarray(inputs["L0_ffn_Wg"], f32)
    Wu = np.asarray(inputs["L0_ffn_Wu"], f32)
    Wd = np.asarray(inputs["L0_ffn_Wd"], f32)
    cw = np.asarray(inputs["L0_attn_conv_w"], f32)
    cb = np.asarray(inputs["L0_attn_conv_b"], f32)

    def blocked(W):
        return np.ascontiguousarray(
            W.reshape(8, 16, 128, 16, 128).transpose(0, 3, 2, 1, 4)
            .reshape(16384, 2048))

    d = {
        "wr": np.ascontiguousarray(
            Wr.reshape(16, 128, 8).transpose(1, 0, 2).reshape(128, 128)),
        "w_inTz": np.ascontiguousarray(ipw[4096:].T),
        "w_inTxb": np.ascontiguousarray(
            ipw[:4096].reshape(32, 128, 16, 128).transpose(0, 3, 2, 1)
            .reshape(4096, 2048)),
        "w_xpT": np.ascontiguousarray(xpw.T),
        "w_outT": np.ascontiguousarray(opw.T),
        "w_dtT": np.ascontiguousarray(
            np.asarray(inputs["L0_attn_dt_proj_w"], f32).T),
        "wgb": blocked(Wg),
        "wub": blocked(Wu),
        "wdb": blocked(Wd),
        "convw": np.ascontiguousarray(
            cw.reshape(32, 128, 4).transpose(1, 0, 2).reshape(128, 128)),
        "convb": np.ascontiguousarray(cb.reshape(32, 128).T),
        "dtb": np.asarray(inputs["L0_attn_dt_proj_b"], f32).reshape(64, 1),
        "alog": np.asarray(inputs["L0_attn_A_log"], f32).reshape(64, 1),
        "dvec": np.asarray(inputs["L0_attn_D"], f32).reshape(64, 1),
        "negi64": (-np.eye(64)).astype(f32),
        "tri": np.where(np.arange(128)[:, None] <= np.arange(128)[None, :],
                        0.0, -1e30).astype(ml_dtypes.bfloat16),
        "id128f": np.eye(128, dtype=f32),
        "id128b": np.eye(128, dtype=ml_dtypes.bfloat16),
        "id64f": np.eye(64, dtype=f32),
        "ones64": np.ones((64, 128), f32),
        "ones64b": np.ones((64, 128), ml_dtypes.bfloat16),
    }
    sel8 = np.zeros((8, 1024), ml_dtypes.bfloat16)
    for e in range(8):
        sel8[e, e * 128:(e + 1) * 128] = 1.0
    d["sel8"] = sel8
    d["epsc"] = np.full((128, 1), 1e-5, f32)
    return d


def kernel(**inputs):
    nc = _get_prog()
    common = _common_inputs(inputs)
    x = np.asarray(inputs["L0_attn_x"], np.float32)
    in_maps = []
    for c in range(NCORES):
        m = dict(common)
        m["x"] = np.ascontiguousarray(x[2 * c:2 * c + 2].reshape(TOK, DM))
        in_maps.append(m)
    res = run_bass_kernel_spmd(nc, in_maps, core_ids=list(range(NCORES)))
    global LAST_RES
    LAST_RES = res
    outs = [res.results[c]["out"].reshape(2, 2048, DM) for c in range(NCORES)]
    return np.concatenate(outs, axis=0)
